# version 02
# baseline (speedup 1.0000x reference)
import numpy as np
import ml_dtypes
import contextlib
import os as _os
import concourse.bass as bass
import concourse.mybir as mybir
from concourse.bass_utils import run_bass_kernel_spmd
from concourse.aot_env import allow_ici_all_to_all

F32 = mybir.dt.float32
BF16 = mybir.dt.bfloat16
AF = mybir.ActivationFunctionType
OP = mybir.AluOpType
AX = mybir.AxisListType
bf16 = ml_dtypes.bfloat16

NCORE = 8
T = 8192
H = 4096
D = 8192
NST = 16
KCV = 4
RDT = 256
TLOC = 1024
G = 256
NGRP = 4

def groups_of(c):
    return [c, 15 - c, 16 + c, 31 - c]


class Ctx:
    def __init__(self, nc, es):
        self.nc = nc
        self.es = es
        self.esem = {}
        self.ecnt = {}
        self.dsem = {}
        self.dcnt = {}
        self.waited = {}

    def bump(self, eng_name, instr, n=1):
        if eng_name not in self.esem:
            self.esem[eng_name] = self.es.enter_context(self.nc.semaphore(f"se_{eng_name}"))
            self.ecnt[eng_name] = 0
        self.ecnt[eng_name] += n
        instr.then_inc(self.esem[eng_name], n)
        return ("E", eng_name, self.ecnt[eng_name])

    def dma(self, stream, instr):
        if stream not in self.dsem:
            self.dsem[stream] = self.es.enter_context(self.nc.semaphore(f"sd_{stream}"))
            self.dcnt[stream] = 0
        self.dcnt[stream] += 16
        instr.then_inc(self.dsem[stream], 16)
        return ("D", stream, self.dcnt[stream])

    def wait(self, eng_obj, waiter_name, milestone):
        kind, key, val = milestone
        if kind == "E" and key == waiter_name:
            return
        wkey = (waiter_name, kind, key)
        if self.waited.get(wkey, -1) >= val:
            return
        sem = self.esem[key] if kind == "E" else self.dsem[key]
        eng_obj.wait_ge(sem, val)
        self.waited[wkey] = val

    def dma_total(self, stream):
        return ("D", stream, self.dcnt[stream])

    def self_edge(self, eng_name, eng_obj, instr):
        m = self.bump(eng_name, instr)
        eng_obj.wait_ge(self.esem[eng_name], m[2])
        return m


def build_m1():
    nc = bass.Bass("TRN2", target_bir_lowering=False, debug=False, detect_race_conditions=not _os.environ.get("BASS_NO_RD"))
    x0T = nc.dram_tensor("x0T", [H, TLOC], F32, kind="ExternalInput").ap()
    x0Tpre = nc.dram_tensor("x0Tpre", [H, 16], F32, kind="ExternalInput").ap()
    W_inT = nc.dram_tensor("W_inT", [H, 2 * D], F32, kind="ExternalInput").ap()
    WxT = nc.dram_tensor("WxT", [D, 288], F32, kind="ExternalInput").ap()
    conv_w = nc.dram_tensor("conv_w", [D, KCV], F32, kind="ExternalInput").ap()
    conv_b2 = nc.dram_tensor("conv_b2", [128, 64], F32, kind="ExternalInput").ap()
    dtlnw2 = nc.dram_tensor("dtlnw2", [128, 2], F32, kind="ExternalInput").ap()
    b_ln_w = nc.dram_tensor("b_ln_w", [NST], F32, kind="ExternalInput").ap()
    c_ln_w = nc.dram_tensor("c_ln_w", [NST], F32, kind="ExternalInput").ap()
    ones_col = nc.dram_tensor("ones_col", [128, 1], F32, kind="ExternalInput").ap()
    ones_row = nc.dram_tensor("ones_row", [1, 128], F32, kind="ExternalInput").ap()
    v_out = nc.dram_tensor("v_out", [D, TLOC], BF16, kind="ExternalOutput").ap()
    gate_out = nc.dram_tensor("gate_out", [D, TLOC], BF16, kind="ExternalOutput").ap()
    dtn_out = nc.dram_tensor("dtn_out", [288, TLOC], BF16, kind="ExternalOutput").ap()

    NF = 128
    NE = 32

    es = contextlib.ExitStack()
    with es:
        x0bf = es.enter_context(nc.sbuf_tensor("x0bf", [128, NE, TLOC], BF16))
        x0pre_bf = es.enter_context(nc.sbuf_tensor("x0pre_bf", [128, NE, 16], BF16))
        wbuf = es.enter_context(nc.sbuf_tensor("wbuf", [128, 2, NE, 128], BF16))
        wxbuf = es.enter_context(nc.sbuf_tensor("wxbuf", [128, 64, 288], BF16))
        cwb = es.enter_context(nc.sbuf_tensor("cwb", [128, 64, KCV], F32))
        cbb = es.enter_context(nc.sbuf_tensor("cbb", [128, 64], F32))
        lnw = es.enter_context(nc.sbuf_tensor("lnw", [128, 4], F32))
        onc = es.enter_context(nc.sbuf_tensor("onc", [128, 1], F32))
        onr = es.enter_context(nc.sbuf_tensor("onr", [1, 128], F32))
        u_pad = es.enter_context(nc.sbuf_tensor("u_pad", [128, NGRP, G + 3], F32))
        vt = es.enter_context(nc.sbuf_tensor("vt", [128, NGRP, G], F32))
        v_sb = es.enter_context(nc.sbuf_tensor("v_sb", [128, 2, TLOC], BF16))
        g_sb = es.enter_context(nc.sbuf_tensor("g_sb", [128, 2, TLOC], BF16))
        vwx = es.enter_context(nc.sbuf_tensor("vwx", [128, 32, 512], BF16))
        ssm_raw = es.enter_context(nc.sbuf_tensor("ssm_raw", [128, 2, 512], F32))
        bc_raw = es.enter_context(nc.sbuf_tensor("bc_raw", [16, 2, 512], F32))
        sq = es.enter_context(nc.sbuf_tensor("sq", [128, 2, 512], F32))
        sqbc = es.enter_context(nc.sbuf_tensor("sqbc", [16, 2, 512], F32))
        rrow = es.enter_context(nc.sbuf_tensor("rrow", [1, 10, 512], F32))
        dtn_sb = es.enter_context(nc.sbuf_tensor("dtn_sb", [128, 3, 512], BF16))
        pproj = es.enter_context(nc.psum_tensor("pproj", [128, 1040], F32))
        pwx1 = es.enter_context(nc.psum_tensor("pwx1", [128, 512], F32))
        pwx2 = es.enter_context(nc.psum_tensor("pwx2", [128, 512], F32))
        pwxbc = es.enter_context(nc.psum_tensor("pwxbc", [64, 512], F32))
        pstat = es.enter_context(nc.psum_tensor("pstat", [1, 512], F32))
        pbcast = es.enter_context(nc.psum_tensor("pbcast", [128, 512], F32))
        block = es.enter_context(nc.Block())
        C = Ctx(nc, es)
        M = {}

        @block.gpsimd
        def _(gp):
            C.dma("gcst", gp.dma_start(out=x0bf[:], in_=x0T.rearrange("(e p) t -> p e t", p=128)))
            C.dma("gcst", gp.dma_start(out=x0pre_bf[:, :, 0:12], in_=x0Tpre[:, 0:12].rearrange("(e p) t -> p e t", p=128)))
            C.dma("gcst", gp.dma_start(out=wxbuf[:], in_=WxT.rearrange("(f p) r -> p f r", p=128)))

        @block.sync
        def _(sp):
            C.dma("cst", sp.dma_start(out=cwb[:], in_=conv_w.rearrange("(f p) k -> p f k", p=128)))
            C.dma("cst", sp.dma_start(out=cbb[:], in_=conv_b2[:]))
            C.dma("cst", sp.dma_start(out=lnw[:, 0:2], in_=dtlnw2[:]))
            C.dma("cst", sp.dma_start(out=lnw[0:16, 2:3], in_=b_ln_w.rearrange("(n o) -> n o", o=1)))
            C.dma("cst", sp.dma_start(out=lnw[0:16, 3:4], in_=c_ln_w.rearrange("(n o) -> n o", o=1)))
            C.dma("cst", sp.dma_start(out=onc[:], in_=ones_col[:]))
            C.dma("cst", sp.dma_start(out=onr[:], in_=ones_row[:]))
        CST = C.dma_total("cst")
        GCST = C.dma_total("gcst")

        ev_mark = {}
        for f in range(NF):
            uhalf = f < 64
            wstream = f"wb{f % 2}"

            @block.gpsimd
            def _(gp, f=f, wstream=wstream):
                if f >= 2:
                    C.wait(gp, "pool", M[f"pe_{f-2}"])
                M[f"w_{f}"] = C.dma(wstream, gp.dma_start(
                    out=wbuf[:, f % 2],
                    in_=W_inT[:, 128 * f:128 * (f + 1)].rearrange("(e p) j -> p e j", p=128)))

            @block.tensor
            def _(te, f=f, uhalf=uhalf):
                C.wait(te, "pe", M[f"w_{f}"])
                if f == 0:
                    C.wait(te, "pe", CST)
                    C.wait(te, "pe", GCST)
                if f >= 1:
                    C.wait(te, "pe", ev_mark[f - 1])
                last = None
                for e in range(NE):
                    lhsT = wbuf[:, f % 2, e]
                    te.ldweights(lhsT)
                    te.matmul(pproj[:, 0:512], lhsT, x0bf[:, e, 0:512], start=(e == 0), stop=(e == NE - 1))
                    last = te.matmul(pproj[:, 512:1024], lhsT, x0bf[:, e, 512:1024], start=(e == 0), stop=(e == NE - 1))
                    if uhalf:
                        last = te.matmul(pproj[:, 1024:1036], lhsT, x0pre_bf[:, e, 0:12], start=(e == 0), stop=(e == NE - 1))
                M[f"pe_{f}"] = C.bump("pe", last)

            @block.scalar
            def _(sc, f=f, uhalf=uhalf):
                C.wait(sc, "act", M[f"pe_{f}"])
                if uhalf:
                    if f >= 1:
                        C.wait(sc, "act", M[f"conv_{f-1}"])
                    last = None
                    for g in range(NGRP):
                        sc.copy(u_pad[:, g, 3:3 + G], pproj[:, G * g:G * (g + 1)])
                        last = sc.copy(u_pad[:, g, 0:3], pproj[:, 1024 + 3 * g:1024 + 3 * g + 3])
                    ev_mark[f] = M[f"ev_{f}"] = C.bump("act", last)
                else:
                    fb = f - 64
                    if f >= 66:
                        C.wait(sc, "act", M[f"gst_{f-2}"])
                    half = fb % 2
                    sc.copy(g_sb[:, half, 0:512], pproj[:, 0:512])
                    ev_mark[f] = M[f"gev_{f}"] = C.bump("act", sc.copy(g_sb[:, half, 512:1024], pproj[:, 512:1024]))

            if uhalf:
                @block.vector
                def _(dv, f=f):
                    C.wait(dv, "dve", M[f"ev_{f}"])
                    C.wait(dv, "dve", CST)
                    if f >= 1:
                        C.wait(dv, "dve", M[f"vs_{f-1}"])
                    dv.tensor_scalar(vt[:], u_pad[:, :, 0:G], cwb[:, f, 0:1], None, OP.mult)
                    last = None
                    for k in range(1, KCV):
                        last = dv.scalar_tensor_tensor(vt[:], u_pad[:, :, k:k + G], cwb[:, f, k:k + 1], vt[:], OP.mult, OP.add)
                    M[f"conv_{f}"] = C.bump("dve", last)

                @block.scalar
                def _(sc, f=f):
                    C.wait(sc, "act", M[f"conv_{f}"])
                    if f >= 2:
                        C.wait(sc, "act", M[f"vst_{f-2}"])
                    M[f"vs_{f}"] = C.bump("act", sc.activation(
                        v_sb[:, f % 2].rearrange("p (g t) -> p g t", g=NGRP), vt[:],
                        AF.Silu, bias=cbb[:, f:f + 1], scale=1.0))

            @block.sync
            def _(sp, f=f, uhalf=uhalf):
                if uhalf:
                    C.wait(sp, "sp", M[f"vs_{f}"])
                    M[f"vst_{f}"] = C.dma(f"vst{f % 2}", sp.dma_start(out=v_out[128 * f:128 * (f + 1), :], in_=v_sb[:, f % 2]))
                else:
                    C.wait(sp, "sp", M[f"gev_{f}"])
                    M[f"gst_{f}"] = C.dma(f"gst{(f - 64) % 2}", sp.dma_start(out=gate_out[128 * (f - 64):128 * (f - 63), :], in_=g_sb[:, (f - 64) % 2]))

        for t in range(2):
            for fh in range(2):
                @block.sync
                def _(sp, t=t, fh=fh):
                    C.wait(sp, "sp", C.dma_total("vst0"))
                    C.wait(sp, "sp", C.dma_total("vst1"))
                    if t > 0 or fh > 0:
                        C.wait(sp, "sp", M[f"wx_mm_{t}_{fh-1}"] if fh > 0 else M[f"wx_mm_{t-1}_1"])
                    M[f"vwx_{t}_{fh}"] = C.dma("vwx", sp.dma_start(
                        out=vwx[:], in_=v_out[4096 * fh:4096 * (fh + 1), 512 * t:512 * (t + 1)].rearrange("(f p) x -> p f x", p=128)))

                @block.tensor
                def _(te, t=t, fh=fh):
                    C.wait(te, "pe", M[f"vwx_{t}_{fh}"])
                    if t == 1 and fh == 0:
                        C.wait(te, "pe", M["nrm_done_0"])
                    last = None
                    for fi in range(32):
                        f = 32 * fh + fi
                        st = (f == 0)
                        sto = (f == 63)
                        te.ldweights(wxbuf[:, f, 0:128])
                        te.matmul(pwx1[:], wxbuf[:, f, 0:128], vwx[:, fi], start=st, stop=sto)
                        te.ldweights(wxbuf[:, f, 128:256])
                        te.matmul(pwx2[:], wxbuf[:, f, 128:256], vwx[:, fi], start=st, stop=sto)
                        te.ldweights(wxbuf[:, f, 256:272])
                        te.matmul(pwxbc[0:16, :], wxbuf[:, f, 256:272], vwx[:, fi], start=st, stop=sto)
                        te.ldweights(wxbuf[:, f, 272:288])
                        last = te.matmul(pwxbc[32:48, :], wxbuf[:, f, 272:288], vwx[:, fi], start=st, stop=sto)
                    M[f"wx_mm_{t}_{fh}"] = C.bump("pe", last)
            M[f"wx_mm_{t}"] = M[f"wx_mm_{t}_1"]

            @block.scalar
            def _(sc, t=t):
                C.wait(sc, "act", M[f"wx_mm_{t}"])
                if t == 1:
                    C.wait(sc, "act", M["dtnst_dt_0"])
                sc.copy(ssm_raw[:, 0], pwx1[:])
                sc.copy(ssm_raw[:, 1], pwx2[:])
                sc.copy(bc_raw[:, 0], pwxbc[0:16])
                sc.copy(bc_raw[:, 1], pwxbc[32:48])
                sc.activation(sq[:, 0], pwx1[:], AF.Square)
                sc.activation(sq[:, 1], pwx2[:], AF.Square)
                sc.activation(sqbc[:, 0], pwxbc[0:16], AF.Square)
                M[f"sqd_{t}"] = C.bump("act", sc.activation(sqbc[:, 1], pwxbc[32:48], AF.Square))

            @block.tensor
            def _(te, t=t):
                C.wait(te, "pe", M[f"sqd_{t}"])
                te.matmul(pstat[:], onc[:, 0:1], sq[:, 0], start=True, stop=False)
                M[f"st_dt_{t}"] = C.bump("pe", te.matmul(pstat[:], onc[:, 0:1], sq[:, 1], start=False, stop=True))

            @block.vector
            def _(dv, t=t):
                C.wait(dv, "dve", M[f"st_dt_{t}"])
                M[f"ms_dt_{t}"] = C.bump("dve", dv.tensor_scalar(rrow[:, 0], pstat[:], 1.0 / 256, 1e-6, OP.mult, OP.add))

            @block.scalar
            def _(sc, t=t):
                C.wait(sc, "act", M[f"ms_dt_{t}"])
                M[f"sqrt_dt_{t}"] = C.bump("act", sc.activation(rrow[:, 1], rrow[:, 0], AF.Sqrt))

            @block.vector
            def _(dv, t=t):
                C.wait(dv, "dve", M[f"sqrt_dt_{t}"])
                M[f"rstd_dt_{t}"] = C.bump("dve", dv.reciprocal(rrow[:, 2], rrow[:, 1]))

            @block.tensor
            def _(te, t=t):
                C.wait(te, "pe", M[f"rstd_dt_{t}"])
                M[f"bc_dt_{t}"] = C.bump("pe", te.matmul(pbcast[:], onr[:], rrow[:, 2], start=True, stop=True))

            @block.vector
            def _(dv, t=t):
                C.wait(dv, "dve", M[f"bc_dt_{t}"])
                if t == 1:
                    C.wait(dv, "dve", M["dtnst_dt_0"])
                dv.scalar_tensor_tensor(dtn_sb[:, 0], ssm_raw[:, 0], lnw[:, 0:1], pbcast[:], OP.mult, OP.mult)
                M[f"dtn_dt_{t}"] = C.bump("dve", dv.scalar_tensor_tensor(
                    dtn_sb[:, 1], ssm_raw[:, 1], lnw[:, 1:2], pbcast[:], OP.mult, OP.mult))

            @block.sync
            def _(sp, t=t):
                C.wait(sp, "sp", M[f"dtn_dt_{t}"])
                M[f"dtnst_dt_{t}"] = C.dma("dtst", sp.dma_start(
                    out=dtn_out[0:256, 512 * t:512 * (t + 1)].rearrange("(a p) x -> p a x", p=128), in_=dtn_sb[:, 0:2]))

            for bi, lncol in enumerate((2, 3)):
                @block.tensor
                def _(te, t=t, bi=bi):
                    C.wait(te, "pe", M[f"sqd_{t}"])
                    C.wait(te, "pe", M[f"ms_dt_{t}"] if bi == 0 else M[f"ms_b0_{t}"])
                    M[f"st_b{bi}_{t}"] = C.bump("pe", te.matmul(pstat[:], onc[0:16, 0:1], sqbc[:, bi], start=True, stop=True))

                @block.vector
                def _(dv, t=t, bi=bi):
                    C.wait(dv, "dve", M[f"st_b{bi}_{t}"])
                    M[f"ms_b{bi}_{t}"] = C.bump("dve", dv.tensor_scalar(rrow[:, 3 + 3 * bi], pstat[:], 1.0 / 16, 1e-6, OP.mult, OP.add))

                @block.scalar
                def _(sc, t=t, bi=bi):
                    C.wait(sc, "act", M[f"ms_b{bi}_{t}"])
                    M[f"sqrt_b{bi}_{t}"] = C.bump("act", sc.activation(rrow[:, 4 + 3 * bi], rrow[:, 3 + 3 * bi], AF.Sqrt))

                @block.vector
                def _(dv, t=t, bi=bi):
                    C.wait(dv, "dve", M[f"sqrt_b{bi}_{t}"])
                    M[f"rstd_b{bi}_{t}"] = C.bump("dve", dv.reciprocal(rrow[:, 5 + 3 * bi], rrow[:, 4 + 3 * bi]))

                @block.tensor
                def _(te, t=t, bi=bi):
                    C.wait(te, "pe", M[f"rstd_b{bi}_{t}"])
                    C.wait(te, "pe", M[f"dtn_dt_{t}"] if bi == 0 else M[f"dtn_b0_{t}"])
                    M[f"bc_b{bi}_{t}"] = C.bump("pe", te.matmul(pbcast[:], onr[:], rrow[:, 5 + 3 * bi], start=True, stop=True))

                @block.vector
                def _(dv, t=t, bi=bi, lncol=lncol):
                    C.wait(dv, "dve", M[f"bc_b{bi}_{t}"])
                    if bi == 1:
                        C.wait(dv, "dve", M[f"dtnst_b0_{t}"])
                    elif t == 1:
                        C.wait(dv, "dve", M["dtnst_b1_0"])
                    M[f"dtn_b{bi}_{t}"] = C.bump("dve", dv.scalar_tensor_tensor(
                        dtn_sb[0:16, 2], bc_raw[:, bi], lnw[0:16, lncol:lncol + 1], pbcast[0:16, :], OP.mult, OP.mult))

                @block.sync
                def _(sp, t=t, bi=bi):
                    C.wait(sp, "sp", M[f"dtn_b{bi}_{t}"])
                    M[f"dtnst_b{bi}_{t}"] = C.dma(f"bcst{bi}", sp.dma_start(
                        out=dtn_out[256 + 16 * bi:256 + 16 * (bi + 1), 512 * t:512 * (t + 1)], in_=dtn_sb[0:16, 2]))

            M[f"nrm_done_{t}"] = M[f"dtn_b1_{t}"]

        @block.gpsimd
        def _(gp):
            for s in ("vst0", "vst1", "gst0", "gst1", "dtst", "bcst0", "bcst1"):
                C.wait(gp, "pool", C.dma_total(s))
    return nc


def build_m2():
    nc = bass.Bass("TRN2", target_bir_lowering=False, debug=False, detect_race_conditions=not _os.environ.get("BASS_NO_RD"))
    v_m2 = nc.dram_tensor("v_m2", [1024, T], BF16, kind="ExternalInput").ap()
    dtn_dt = nc.dram_tensor("dtn_dt", [RDT, T], BF16, kind="ExternalInput").ap()
    dtn_b = nc.dram_tensor("dtn_b", [NST, T], BF16, kind="ExternalInput").ap()
    dtn_c = nc.dram_tensor("dtn_c", [NST, T], BF16, kind="ExternalInput").ap()
    wdtT_c = nc.dram_tensor("wdtT_c", [RDT, 1024], F32, kind="ExternalInput").ap()
    A_c = nc.dram_tensor("A_c", [128, 8 * NST], F32, kind="ExternalInput").ap()
    dtb_c = nc.dram_tensor("dtb_c", [128, 8], F32, kind="ExternalInput").ap()
    Dp_c = nc.dram_tensor("Dp_c", [128, 8], F32, kind="ExternalInput").ap()
    esel = nc.dram_tensor("esel", [NST, NST * 128], BF16, kind="ExternalInput").ap()
    cst = nc.dram_tensor("cst", [128, 4], F32, kind="ExternalInput").ap()
    y2_out = nc.dram_tensor("y2_out", [1024, T], BF16, kind="ExternalOutput").ap()
    dbg = _os.environ.get("M2_DEBUG")
    if dbg:
        delta_out = nc.dram_tensor("delta_out", [1024, 2048], BF16, kind="ExternalOutput").ap()
        du_out = nc.dram_tensor("du_out", [1024, 2048], BF16, kind="ExternalOutput").ap()
        bbc_out = nc.dram_tensor("bbc_out", [128, 2048], BF16, kind="ExternalOutput").ap()
        a_out = nc.dram_tensor("a_out", [128, 2048], F32, kind="ExternalOutput").ap()
        s_out = nc.dram_tensor("s_out", [128, 2048], BF16, kind="ExternalOutput").ap()

    NSPAN = int(_os.environ.get("M2_NSPAN", "4"))
    NSTL = int(_os.environ.get("M2_NST", str(NST)))
    SP = 2048

    es = contextlib.ExitStack()
    with es:
        vh = es.enter_context(nc.sbuf_tensor("vh", [128, 4, SP], BF16))
        dtn_sb = es.enter_context(nc.sbuf_tensor("dtn_sb2", [128, 2, SP], BF16))
        bt = es.enter_context(nc.sbuf_tensor("bt", [NST, SP], BF16))
        ct = es.enter_context(nc.sbuf_tensor("ct", [NST, SP], BF16))
        wdt = es.enter_context(nc.sbuf_tensor("wdt", [128, 2, 8, 128], BF16))
        ab = es.enter_context(nc.sbuf_tensor("ab", [128, 8 * NST], F32))
        dtb = es.enter_context(nc.sbuf_tensor("dtb", [128, 8], F32))
        dpp = es.enter_context(nc.sbuf_tensor("dpp", [128, 8], F32))
        esl = es.enter_context(nc.sbuf_tensor("esl", [NST, NST * 128], BF16))
        cstb = es.enter_context(nc.sbuf_tensor("cstb2", [128, 4], F32))
        delta = es.enter_context(nc.sbuf_tensor("delta", [128, 4, SP], BF16))
        du = es.enter_context(nc.sbuf_tensor("du", [128, 4, SP], BF16))
        sp_t = es.enter_context(nc.sbuf_tensor("sp_t", [128, SP], F32))
        sp_t2 = es.enter_context(nc.sbuf_tensor("sp_t2", [128, SP], F32))
        bbc = es.enter_context(nc.sbuf_tensor("bbc", [128, SP], BF16))
        cbc = es.enter_context(nc.sbuf_tensor("cbc", [128, SP], BF16))
        a_sb = es.enter_context(nc.sbuf_tensor("a_sb", [128, 2, SP], F32))
        b_sb = es.enter_context(nc.sbuf_tensor("b_sb", [128, 2, SP], BF16))
        s_sb = es.enter_context(nc.sbuf_tensor("s_sb", [128, 2, SP], BF16))
        z_sb = es.enter_context(nc.sbuf_tensor("z_sb", [128, 2, SP], BF16))
        y_acc = es.enter_context(nc.sbuf_tensor("y_acc", [128, 4, SP], F32))
        y2_sb = es.enter_context(nc.sbuf_tensor("y2_sb", [128, 2, SP], BF16))
        carrys = [es.enter_context(nc.sbuf_tensor(f"cr{i}", [128, 1], F32)) for i in range(8 * NST)]
        pdelta = es.enter_context(nc.psum_tensor("pdelta", [128, 2, 512], F32))
        pbc = es.enter_context(nc.psum_tensor("pbc", [128, SP], F32))
        block = es.enter_context(nc.Block())
        C = Ctx(nc, es)
        M = {}

        @block.gpsimd
        def _(gp):
            C.dma("wdtl", gp.dma_start(out=wdt[:], in_=wdtT_c.rearrange("(rc p) (dt j) -> p rc dt j", p=128, j=128)))

        @block.sync
        def _(sp):
            C.dma("cst", sp.dma_start(out=ab[:], in_=A_c[:]))
            C.dma("cst", sp.dma_start(out=dtb[:], in_=dtb_c[:]))
            C.dma("cst", sp.dma_start(out=dpp[:], in_=Dp_c[:]))
            C.dma("cst", sp.dma_start(out=esl[:], in_=esel[:]))
            C.dma("cst", sp.dma_start(out=cstb[:], in_=cst[:]))
        CST = C.dma_total("cst")
        WDTL = C.dma_total("wdtl")

        pd_free = {}
        a_free = {}
        a_last = {}
        z_free = {}
        last_bbc_read = None
        last_pbc_ev = None
        prev_delta_add = None
        prev_y2 = {}
        y2st = {}

        for s in range(NSPAN):
            tsl = slice(SP * s, SP * (s + 1))
            @block.sync
            def _(sp, s=s, tsl=tsl):
                if s >= 1:
                    C.wait(sp, "sp", M[f"dmm_last_{s-1}"])
                    C.wait(sp, "sp", M[f"bc_mm_last_{s-1}"])
                C.dma("dtnl", sp.dma_start(
                    out=dtn_sb[:], in_=dtn_dt[:, tsl].rearrange("(a p) t -> p a t", p=128)))
                C.dma("dtnl", sp.dma_start(out=bt[:], in_=dtn_b[:, tsl]))
                C.dma("dtnl", sp.dma_start(out=ct[:], in_=dtn_c[:, tsl]))
                M[f"dtnl_{s}"] = C.dma_total("dtnl")

            for h in range(2):
                @block.sync
                def _(sp, s=s, h=h, tsl=tsl):
                    prev = y2st.get((s, h - 1, 3) if h else (s - 1, 1, 3))
                    if prev is not None:
                        C.wait(sp, "sp", prev)
                    M[f"vl_{s}_{h}"] = C.dma("vhl", sp.dma_start(
                        out=vh[:], in_=v_m2[512 * h:512 * (h + 1), tsl].rearrange("(dt p) t -> p dt t", p=128)))

                for di in range(4):
                    dt_g = 4 * h + di
                    for q in range(4):
                        par = q % 2
                        @block.tensor
                        def _(te, s=s, di=di, dt_g=dt_g, q=q, par=par):
                            C.wait(te, "pe", M[f"dtnl_{s}"])
                            C.wait(te, "pe", CST)
                            C.wait(te, "pe", WDTL)
                            if par in pd_free:
                                C.wait(te, "pe", pd_free[par])
                            te.matmul(pdelta[:, par], wdt[:, 0, dt_g], dtn_sb[:, 0, 512 * q:512 * (q + 1)], start=True, stop=False)
                            M["dmm_cur"] = C.bump("pe", te.matmul(
                                pdelta[:, par], wdt[:, 1, dt_g], dtn_sb[:, 1, 512 * q:512 * (q + 1)], start=False, stop=True))

                        @block.scalar
                        def _(sc, s=s, h=h, di=di, dt_g=dt_g, q=q, par=par):
                            C.wait(sc, "act", CST)
                            C.wait(sc, "act", M["dmm_cur"])
                            if q == 0 and prev_delta_add is not None:
                                C.wait(sc, "act", prev_delta_add)
                            qq = slice(512 * q, 512 * (q + 1))
                            sc.activation(sp_t[:, qq], pdelta[:, par], AF.Abs, bias=dtb[:, dt_g:dt_g + 1])
                            sc.activation(sp_t[:, qq], sp_t[:, qq], AF.Exp, scale=-1.0)
                            sc.activation(sp_t[:, qq], sp_t[:, qq], AF.Ln, bias=cstb[:, 0:1])
                            pd_free[par] = M["spq_cur"] = C.bump("act", sc.activation(
                                sp_t2[:, qq], pdelta[:, par], AF.Relu, bias=dtb[:, dt_g:dt_g + 1]))
                    M[f"spl_{s}_{h}_{di}"] = M["spq_cur"]
                    if di == 3 and h == 1:
                        M[f"dmm_last_{s}"] = M["dmm_cur"]

                    @block.vector
                    def _(dv, s=s, h=h, di=di):
                        C.wait(dv, "dve", M[f"spl_{s}_{h}_{di}"])
                        if di in a_last:
                            C.wait(dv, "dve", a_last[di])
                        M[f"delta_{s}_{h}_{di}"] = C.bump("dve", dv.tensor_tensor(delta[:, di], sp_t2[:], sp_t[:], OP.add))
                        C.wait(dv, "dve", M[f"vl_{s}_{h}"])
                        M[f"du_{s}_{h}_{di}"] = C.bump("dve", dv.tensor_tensor(du[:, di], delta[:, di], vh[:, di], OP.mult))
                    prev_delta_add = M[f"delta_{s}_{h}_{di}"]

                    if dbg and s == 0:
                        @block.sync
                        def _(sp, h=h, di=di):
                            C.wait(sp, "sp", M[f"du_0_{h}_{di}"])
                            C.dma(f"dbg{h}_{di}", sp.dma_start(out=delta_out[512 * h + 128 * di:512 * h + 128 * (di + 1), :], in_=delta[:, di]))
                            C.dma(f"dbg{h}_{di}", sp.dma_start(out=du_out[512 * h + 128 * di:512 * h + 128 * (di + 1), :], in_=du[:, di]))

                for n in range(NSTL):
                    for which, (row_src, dstb) in enumerate(((bt, bbc), (ct, cbc))):
                        @block.tensor
                        def _(te, s=s, n=n, row_src=row_src, which=which):
                            C.wait(te, "pe", M[f"dtnl_{s}"])
                            if last_pbc_ev is not None:
                                C.wait(te, "pe", last_pbc_ev)
                            last = None
                            for q in range(4):
                                last = te.matmul(pbc[:, 512 * q:512 * (q + 1)], esl[:, 128 * n:128 * (n + 1)],
                                                 row_src[:, 512 * q:512 * (q + 1)], start=True, stop=True)
                            M["bc_mm_cur"] = C.bump("pe", last)

                        @block.scalar
                        def _(sc, dstb=dstb):
                            C.wait(sc, "act", M["bc_mm_cur"])
                            if last_bbc_read is not None:
                                C.wait(sc, "act", last_bbc_read)
                            M["bc_ev_cur"] = C.bump("act", sc.copy(dstb[:], pbc[:]))
                        last_pbc_ev = M["bc_ev_cur"]
                        if which == 0:
                            M["bbc_ev"] = M["bc_ev_cur"]
                        else:
                            M["cbc_ev"] = M["bc_ev_cur"]
                    M[f"bc_mm_last_{s}"] = M["bc_mm_cur"]

                    for di in range(4):
                        dt_g = 4 * h + di
                        par = di % 2
                        @block.scalar
                        def _(sc, s=s, h=h, n=n, di=di, dt_g=dt_g, par=par):
                            C.wait(sc, "act", M[f"delta_{s}_{h}_{di}"])
                            if par in a_free:
                                C.wait(sc, "act", a_free[par])
                            M[f"a_{par}"] = C.bump("act", sc.activation(
                                a_sb[:, par], delta[:, di], AF.Exp, scale=ab[:, NST * dt_g + n:NST * dt_g + n + 1]))
                            if n == NSTL - 1:
                                a_last[di] = M[f"a_{par}"]

                        @block.vector
                        def _(dv, s=s, h=h, n=n, di=di, dt_g=dt_g, par=par):
                            C.wait(dv, "dve", M[f"du_{s}_{h}_{di}"])
                            C.wait(dv, "dve", M["bbc_ev"])
                            dv.tensor_tensor(b_sb[:, par], du[:, di], bbc[:], OP.mult)
                            C.wait(dv, "dve", M[f"a_{par}"])
                            cr = carrys[dt_g * NST + n]
                            if s == 0:
                                sc_i = dv.tensor_tensor_scan(s_sb[:, par], a_sb[:, par], b_sb[:, par], 0.0, OP.mult, OP.add)
                            else:
                                sc_i = dv.tensor_tensor_scan(s_sb[:, par], a_sb[:, par], b_sb[:, par], cr[:, 0:1], OP.mult, OP.add)
                            C.self_edge("dve", dv, sc_i)
                            a_free[par] = C.bump("dve", dv.tensor_copy(cr[:], s_sb[:, par, SP - 1:SP]))
                            C.wait(dv, "dve", M["cbc_ev"])
                            if par in z_free:
                                C.wait(dv, "dve", z_free[par])
                            M[f"z_{par}"] = C.bump("dve", dv.tensor_tensor(z_sb[:, par], s_sb[:, par], cbc[:], OP.mult))
                            if dbg and s == 0 and h == 0 and n == 0 and di == 0:
                                C.wait(dv, "dve", ("E", "dve", C.ecnt["dve"]))
                        if dbg and s == 0 and h == 0 and n == 0 and di == 0:
                            @block.sync
                            def _(sp):
                                C.wait(sp, "sp", M["z_0"])
                                C.dma("dbgs", sp.dma_start(out=a_out[:], in_=a_sb[:, 0]))
                                C.dma("dbgs", sp.dma_start(out=s_out[:], in_=s_sb[:, 0]))
                                C.dma("dbgs", sp.dma_start(out=bbc_out[:], in_=bbc[:]))

                        @block.gpsimd
                        def _(gp, s=s, h=h, n=n, di=di, par=par):
                            C.wait(gp, "pool", M[f"z_{par}"])
                            if n == 0:
                                prev = prev_y2.get(di)
                                if prev is not None:
                                    C.wait(gp, "pool", prev)
                                mm = gp.tensor_copy(y_acc[:, di], z_sb[:, par])
                            else:
                                mm = gp.tensor_tensor(y_acc[:, di], y_acc[:, di], z_sb[:, par], OP.add)
                            z_free[par] = C.bump("pool", mm)
                            if n == NSTL - 1:
                                M[f"y_done_{di}"] = z_free[par]
                    last_bbc_read = M["z_1"]

                for di in range(4):
                    dt_g = 4 * h + di
                    @block.vector
                    def _(dv, s=s, h=h, di=di, dt_g=dt_g):
                        C.wait(dv, "dve", M[f"y_done_{di}"])
                        prevst = y2st.get((s, h, di - 2) if di >= 2 else ((s, h - 1, di + 2) if h else (s - 1, 1, di + 2)))
                        if prevst is not None:
                            C.wait(dv, "dve", prevst)
                        M["y2_cur"] = C.bump("dve", dv.scalar_tensor_tensor(
                            y2_sb[:, di % 2], vh[:, di], dpp[:, dt_g:dt_g + 1], y_acc[:, di], OP.mult, OP.add))
                        prev_y2[di] = M["y2_cur"]

                    @block.sync
                    def _(sp, s=s, h=h, di=di, tsl=tsl):
                        C.wait(sp, "sp", M["y2_cur"])
                        y2st[(s, h, di)] = C.dma(f"y2st{di % 2}", sp.dma_start(
                            out=y2_out[512 * h + 128 * di:512 * h + 128 * (di + 1), tsl], in_=y2_sb[:, di % 2]))

        @block.gpsimd
        def _(gp):
            C.wait(gp, "pool", C.dma_total("y2st0"))
            C.wait(gp, "pool", C.dma_total("y2st1"))
    return nc


def stage_m1_inputs(d):
    x0 = np.ascontiguousarray(d["L0_attn_hidden_states"].reshape(T, H).T)
    common = dict(
        W_inT=np.ascontiguousarray(d["L0_attn_W_in"].T),
        WxT=np.ascontiguousarray(d["L0_attn_Wx"].T),
        conv_w=np.ascontiguousarray(d["L0_attn_conv_w"]),
        conv_b2=np.ascontiguousarray(d["L0_attn_conv_b"].reshape(64, 128).T),
        dtlnw2=np.ascontiguousarray(d["L0_attn_dt_ln_w"].reshape(2, 128).T),
        b_ln_w=np.ascontiguousarray(d["L0_attn_b_ln_w"]),
        c_ln_w=np.ascontiguousarray(d["L0_attn_c_ln_w"]),
        ones_col=np.ones((128, 1), np.float32),
        ones_row=np.ones((1, 128), np.float32),
    )
    maps = []
    for c in range(NCORE):
        cols, pres = [], []
        for k in groups_of(c):
            cols.append(x0[:, G * k:G * (k + 1)])
            pres.append(np.zeros((H, 3), np.float32) if k == 0 else x0[:, G * k - 3:G * k])
        x0Tc = np.ascontiguousarray(np.concatenate(cols, axis=1))
        pre = np.ascontiguousarray(np.concatenate(pres + [np.zeros((H, 4), np.float32)], axis=1))
        maps.append(dict(common, x0T=x0Tc, x0Tpre=pre))
    return maps


def stage_m2_inputs(d, m1_results):
    v_full = gather_tokens([m1_results[c]["v_out"] for c in range(NCORE)])
    dtn_full = gather_tokens([m1_results[c]["dtn_out"] for c in range(NCORE)])
    WdtT = np.ascontiguousarray(d["L0_attn_Wdt"].T)
    A = d["L0_attn_A"]
    dtb = d["L0_attn_dt_bias"]
    Dp = d["L0_attn_D_param"]
    esel = np.zeros((NST, NST * 128), np.float32)
    for n in range(NST):
        esel[n, 128 * n:128 * (n + 1)] = 1.0
    cst = np.zeros((128, 4), np.float32); cst[:, 0] = 1.0
    maps = []
    for c in range(NCORE):
        rows = slice(1024 * c, 1024 * (c + 1))
        Ac = A[rows].reshape(8, 128, NST).transpose(1, 0, 2).reshape(128, 8 * NST)
        maps.append(dict(
            v_m2=np.ascontiguousarray(v_full[rows]),
            dtn_dt=np.ascontiguousarray(dtn_full[0:256]),
            dtn_b=np.ascontiguousarray(dtn_full[256:272]),
            dtn_c=np.ascontiguousarray(dtn_full[272:288]),
            wdtT_c=np.ascontiguousarray(WdtT[:, rows]),
            A_c=np.ascontiguousarray(Ac.astype(np.float32)),
            dtb_c=np.ascontiguousarray(dtb[rows].reshape(8, 128).T.astype(np.float32)),
            Dp_c=np.ascontiguousarray(Dp[rows].reshape(8, 128).T.astype(np.float32)),
            esel=esel.astype(bf16),
            cst=cst,
        ))
    return maps


def gather_tokens(per_core):
    rows = per_core[0].shape[0]
    out = np.empty((rows, T), per_core[0].dtype)
    for c in range(NCORE):
        for g, k in enumerate(groups_of(c)):
            out[:, G * k:G * (k + 1)] = per_core[c][:, G * g:G * (g + 1)]
    return out


def build_m3():
    nc = bass.Bass("TRN2", target_bir_lowering=False, debug=False, detect_race_conditions=not _os.environ.get("BASS_NO_RD"))
    y2T = nc.dram_tensor("y2T", [D, TLOC], BF16, kind="ExternalInput").ap()
    gateT = nc.dram_tensor("gateT", [D, TLOC], BF16, kind="ExternalInput").ap()
    W_outT = nc.dram_tensor("W_outT", [D, H], F32, kind="ExternalInput").ap()
    x0T = nc.dram_tensor("x0T", [H, TLOC], F32, kind="ExternalInput").ap()
    ones_col = nc.dram_tensor("ones_col", [128, 1], F32, kind="ExternalInput").ap()
    ones_row = nc.dram_tensor("ones_row", [1, 128], F32, kind="ExternalInput").ap()
    x1_out = nc.dram_tensor("x1_out", [H, TLOC], F32, kind="ExternalOutput").ap()
    r_scr = nc.dram_tensor("r_scr", [H, TLOC], F32, kind="Internal").ap()

    es = contextlib.ExitStack()
    with es:
        y3 = es.enter_context(nc.sbuf_tensor("y3", [128, 64, TLOC], BF16))
        t_y2 = es.enter_context(nc.sbuf_tensor("t_y2", [128, 2, TLOC], BF16))
        t_g = es.enter_context(nc.sbuf_tensor("t_g", [128, 2, TLOC], BF16))
        t_sg = es.enter_context(nc.sbuf_tensor("t_sg", [128, TLOC], BF16))
        wbuf = es.enter_context(nc.sbuf_tensor("wbuf3", [128, 1, 64, 128], BF16))
        x0t = es.enter_context(nc.sbuf_tensor("x0t", [128, 2, TLOC], F32))
        r_sb = es.enter_context(nc.sbuf_tensor("r_sb", [128, 2, TLOC], F32))
        sq = es.enter_context(nc.sbuf_tensor("sq3", [128, TLOC], F32))
        rrow = es.enter_context(nc.sbuf_tensor("rrow3", [1, 6, TLOC], F32))
        onc = es.enter_context(nc.sbuf_tensor("onc3", [128, 1], F32))
        onr = es.enter_context(nc.sbuf_tensor("onr3", [1, 128], F32))
        xn = es.enter_context(nc.sbuf_tensor("xn", [128, 2, TLOC], F32))
        pW = es.enter_context(nc.psum_tensor("pW", [128, TLOC], F32))
        psum_s = es.enter_context(nc.psum_tensor("psum_s", [1, TLOC], F32))
        psum_q = es.enter_context(nc.psum_tensor("psum_q", [1, TLOC], F32))
        pbc = es.enter_context(nc.psum_tensor("pbc3", [128, TLOC], F32))
        block = es.enter_context(nc.Block())
        C = Ctx(nc, es)
        M = {}

        @block.sync
        def _(sp):
            C.dma("cst", sp.dma_start(out=onc[:], in_=ones_col[:]))
            C.dma("cst", sp.dma_start(out=onr[:], in_=ones_row[:]))
        CST = C.dma_total("cst")

        for dc in range(64):
            par = dc % 2
            @block.sync
            def _(sp, dc=dc, par=par):
                if dc >= 2:
                    C.wait(sp, "sp", M[f"y3_{dc-2}"])
                C.dma(f"y2l{par}", sp.dma_start(out=t_y2[:, par], in_=y2T[128 * dc:128 * (dc + 1), :]))
                M[f"ld_{dc}"] = C.dma(f"gl{par}", sp.dma_start(out=t_g[:, par], in_=gateT[128 * dc:128 * (dc + 1), :]))

            @block.scalar
            def _(sc, dc=dc, par=par):
                C.wait(sc, "act", C.dma_total(f"y2l{par}"))
                C.wait(sc, "act", C.dma_total(f"gl{par}"))
                if dc >= 1:
                    C.wait(sc, "act", M[f"y3_{dc-1}"])
                M[f"sg_{dc}"] = C.bump("act", sc.activation(t_sg[:], t_g[:, par], AF.Silu))

            @block.vector
            def _(dv, dc=dc, par=par):
                C.wait(dv, "dve", M[f"sg_{dc}"])
                M[f"y3_{dc}"] = C.bump("dve", dv.tensor_tensor(y3[:, dc], t_y2[:, par], t_sg[:], OP.mult))

        for hc in range(32):
            par = hc % 2
            @block.gpsimd
            def _(gp, hc=hc, par=par):
                if hc >= 1:
                    C.wait(gp, "pool", M[f"mm_{hc-1}"])
                M[f"wl_{hc}"] = C.dma("wl0", gp.dma_start(
                    out=wbuf[:, 0],
                    in_=W_outT[:, 128 * hc:128 * (hc + 1)].rearrange("(dc p) j -> p dc j", p=128)))

            @block.sync
            def _(sp, hc=hc, par=par):
                if hc >= 2:
                    C.wait(sp, "sp", M[f"resid_{hc-2}"])
                M[f"x0l_{hc}"] = C.dma(f"x0l{par}", sp.dma_start(out=x0t[:, par], in_=x0T[128 * hc:128 * (hc + 1), :]))

            @block.tensor
            def _(te, hc=hc, par=par):
                C.wait(te, "pe", M[f"wl_{hc}"])
                if hc == 0:
                    for dc in range(64):
                        C.wait(te, "pe", M[f"y3_{dc}"])
                if hc >= 1:
                    C.wait(te, "pe", M[f"ev_{hc-1}"])
                last = None
                for dc in range(64):
                    te.ldweights(wbuf[:, 0, dc])
                    te.matmul(pW[:, 0:512], wbuf[:, 0, dc], y3[:, dc, 0:512], start=(dc == 0), stop=(dc == 63))
                    last = te.matmul(pW[:, 512:1024], wbuf[:, 0, dc], y3[:, dc, 512:1024], start=(dc == 0), stop=(dc == 63))
                M[f"mm_{hc}"] = C.bump("pe", last)

            @block.vector
            def _(dv, hc=hc, par=par):
                C.wait(dv, "dve", M[f"mm_{hc}"])
                C.wait(dv, "dve", M[f"x0l_{hc}"])
                if hc >= 2:
                    C.wait(dv, "dve", M[f"rst_{hc-2}"])
                M[f"resid_{hc}"] = C.bump("dve", dv.tensor_tensor(r_sb[:, par], pW[:], x0t[:, par], OP.add))

            @block.scalar
            def _(sc, hc=hc, par=par):
                C.wait(sc, "act", M[f"resid_{hc}"])
                if hc >= 1:
                    C.wait(sc, "act", M[f"stat_{hc-1}"])
                M[f"ev_{hc}"] = M[f"sq_{hc}"] = C.bump("act", sc.activation(sq[:], r_sb[:, par], AF.Square))

            @block.tensor
            def _(te, hc=hc, par=par):
                C.wait(te, "pe", M[f"sq_{hc}"])
                C.wait(te, "pe", CST)
                te.matmul(psum_s[:, 0:512], onc[:, 0:1], r_sb[:, par, 0:512], start=(hc == 0), stop=(hc == 31))
                te.matmul(psum_s[:, 512:1024], onc[:, 0:1], r_sb[:, par, 512:1024], start=(hc == 0), stop=(hc == 31))
                te.matmul(psum_q[:, 0:512], onc[:, 0:1], sq[:, 0:512], start=(hc == 0), stop=(hc == 31))
                M[f"stat_{hc}"] = C.bump("pe", te.matmul(psum_q[:, 512:1024], onc[:, 0:1], sq[:, 512:1024], start=(hc == 0), stop=(hc == 31)))

            @block.sync
            def _(sp, hc=hc, par=par):
                C.wait(sp, "sp", M[f"stat_{hc}"])
                M[f"rst_{hc}"] = C.dma(f"rst{par}", sp.dma_start(out=r_scr[128 * hc:128 * (hc + 1), :], in_=r_sb[:, par]))

        @block.vector
        def _(dv):
            C.wait(dv, "dve", M["stat_31"])
            C.self_edge("dve", dv, dv.tensor_scalar(rrow[:, 0], psum_s[:], 1.0 / H, None, OP.mult))
            C.self_edge("dve", dv, dv.tensor_scalar(rrow[:, 1], psum_q[:], 1.0 / H, None, OP.mult))
            C.self_edge("dve", dv, dv.tensor_tensor(rrow[:, 2], rrow[:, 0], rrow[:, 0], OP.mult))
            C.self_edge("dve", dv, dv.tensor_tensor(rrow[:, 3], rrow[:, 1], rrow[:, 2], OP.subtract))
            M["var"] = C.bump("dve", dv.tensor_scalar(rrow[:, 3], rrow[:, 3], 1.0, 1e-5, OP.mult, OP.add))

        @block.scalar
        def _(sc):
            C.wait(sc, "act", M["var"])
            M["sqrt"] = C.bump("act", sc.activation(rrow[:, 4], rrow[:, 3], AF.Sqrt))

        @block.vector
        def _(dv):
            C.wait(dv, "dve", M["sqrt"])
            M["rstd"] = C.bump("dve", dv.reciprocal(rrow[:, 5], rrow[:, 4]))

        @block.tensor
        def _(te):
            C.wait(te, "pe", M["rstd"])
            te.matmul(pbc[:, 0:512], onr[:], rrow[:, 5, 0:512], start=True, stop=True)
            te.matmul(pbc[:, 512:1024], onr[:], rrow[:, 5, 512:1024], start=True, stop=True)
            te.matmul(pW[:, 0:512], onr[:], rrow[:, 0, 0:512], start=True, stop=True)
            M["bc"] = C.bump("pe", te.matmul(pW[:, 512:1024], onr[:], rrow[:, 0, 512:1024], start=True, stop=True))

        for hc in range(32):
            par = hc % 2
            @block.sync
            def _(sp, hc=hc, par=par):
                C.wait(sp, "sp", C.dma_total("rst0"))
                C.wait(sp, "sp", C.dma_total("rst1"))
                if hc >= 2:
                    C.wait(sp, "sp", M[f"norm_{hc-2}"])
                M[f"rrd_{hc}"] = C.dma(f"rrd{par}", sp.dma_start(out=r_sb[:, par], in_=r_scr[128 * hc:128 * (hc + 1), :]))

            @block.vector
            def _(dv, hc=hc, par=par):
                C.wait(dv, "dve", M[f"rrd_{hc}"])
                C.wait(dv, "dve", M["bc"])
                if hc >= 2:
                    C.wait(dv, "dve", M[f"xst_{hc-2}"])
                dv.tensor_tensor(xn[:, par], r_sb[:, par], pW[:], OP.subtract)
                M[f"norm_{hc}"] = C.bump("dve", dv.tensor_tensor(xn[:, par], xn[:, par], pbc[:], OP.mult))

            @block.sync
            def _(sp, hc=hc, par=par):
                C.wait(sp, "sp", M[f"norm_{hc}"])
                M[f"xst_{hc}"] = C.dma(f"xst{par}", sp.dma_start(out=x1_out[128 * hc:128 * (hc + 1), :], in_=xn[:, par]))

        @block.gpsimd
        def _(gp):
            C.wait(gp, "pool", C.dma_total("xst0"))
            C.wait(gp, "pool", C.dma_total("xst1"))
    return nc


def stage_m3_inputs(d, m1r, y2_cores):
    y2_all = np.concatenate(y2_cores, axis=0)
    W_outT = np.ascontiguousarray(d["L0_attn_W_out"].T)
    x0 = np.ascontiguousarray(d["L0_attn_hidden_states"].reshape(T, H).T)
    maps = []
    for c in range(NCORE):
        cols = []
        for k in groups_of(c):
            cols.append(y2_all[:, G * k:G * (k + 1)])
        y2T = np.ascontiguousarray(np.concatenate(cols, axis=1))
        x0Tc = np.ascontiguousarray(np.concatenate([x0[:, G * k:G * (k + 1)] for k in groups_of(c)], axis=1))
        maps.append(dict(
            y2T=y2T,
            gateT=np.ascontiguousarray(m1r[c]["gate_out"].view(bf16) if m1r[c]["gate_out"].dtype != bf16 else m1r[c]["gate_out"]),
            W_outT=W_outT,
            x0T=x0Tc,
            ones_col=np.ones((128, 1), np.float32),
            ones_row=np.ones((1, 128), np.float32),
        ))
    return maps


def build_ffn0():
    nc = bass.Bass("TRN2", target_bir_lowering=False, debug=False, detect_race_conditions=not _os.environ.get("BASS_NO_RD"))
    x1T = nc.dram_tensor("x1T", [H, TLOC], F32, kind="ExternalInput").ap()
    base_WT = nc.dram_tensor("base_WT", [H, H], F32, kind="ExternalInput").ap()
    router_WT = nc.dram_tensor("router_WT", [H, 8], F32, kind="ExternalInput").ap()
    loraA2 = nc.dram_tensor("loraA2", [H, 128], F32, kind="ExternalInput").ap()
    loraB2 = nc.dram_tensor("loraB2", [128, H], F32, kind="ExternalInput").ap()
    rep16 = nc.dram_tensor("rep16", [8, 128], F32, kind="ExternalInput").ap()
    ident = nc.dram_tensor("ident", [128, 128], F32, kind="ExternalInput").ap()
    ones_col = nc.dram_tensor("ones_col", [128, 1], F32, kind="ExternalInput").ap()
    ones_row = nc.dram_tensor("ones_row", [1, 128], F32, kind="ExternalInput").ap()
    x2_out = nc.dram_tensor("x2_out", [H, TLOC], F32, kind="ExternalOutput").ap()
    r_scr = nc.dram_tensor("r_scr0", [H, TLOC], F32, kind="Internal").ap()
    f0dbg = _os.environ.get("F0_DEBUG")
    if f0dbg:
        lg_out = nc.dram_tensor("lg_out", [8, TLOC], F32, kind="ExternalOutput").ap()
        wd_out = nc.dram_tensor("wd_out", [8, TLOC], BF16, kind="ExternalOutput").ap()
        w16_out = nc.dram_tensor("w16_out", [128, TLOC], BF16, kind="ExternalOutput").ap()
        midg_out = nc.dram_tensor("midg_out", [128, TLOC], BF16, kind="ExternalOutput").ap()
        base_out = nc.dram_tensor("base_out", [128, TLOC], F32, kind="ExternalOutput").ap()
        r_out = nc.dram_tensor("r_out", [128, TLOC], F32, kind="ExternalOutput").ap()

    es = contextlib.ExitStack()
    with es:
        x1bf = es.enter_context(nc.sbuf_tensor("x1bf", [128, 32, TLOC], BF16))
        bwbuf = es.enter_context(nc.sbuf_tensor("bwbuf", [128, 2, 32, 128], BF16))
        a2bf = es.enter_context(nc.sbuf_tensor("a2bf", [128, 32, 128], BF16))
        b2bf = es.enter_context(nc.sbuf_tensor("b2bf", [128, 32, 128], BF16))
        repb = es.enter_context(nc.sbuf_tensor("repb", [8, 128], BF16))
        idt = es.enter_context(nc.sbuf_tensor("idt", [128, 128], F32))
        onc = es.enter_context(nc.sbuf_tensor("onc0", [128, 1], F32))
        onr = es.enter_context(nc.sbuf_tensor("onr0", [1, 128], F32))
        lgT = es.enter_context(nc.sbuf_tensor("lgT", [8, TLOC], F32))
        lg = es.enter_context(nc.sbuf_tensor("lg", [128, 8, 8], F32))
        esb = es.enter_context(nc.sbuf_tensor("esb", [128, 8, 8], F32))
        wd = es.enter_context(nc.sbuf_tensor("wd", [128, 8, 8], F32))
        mrow = es.enter_context(nc.sbuf_tensor("mrow", [128, 8, 4], F32))
        top8 = es.enter_context(nc.sbuf_tensor("top8", [128, 8, 8], F32))
        wdT = es.enter_context(nc.sbuf_tensor("wdT", [8, TLOC], BF16))
        w16 = es.enter_context(nc.sbuf_tensor("w16", [128, TLOC], BF16))
        midg = es.enter_context(nc.sbuf_tensor("midg", [128, TLOC], BF16))
        fb = es.enter_context(nc.sbuf_tensor("fb", [128, TLOC], F32))
        x1f = es.enter_context(nc.sbuf_tensor("x1f", [128, 2, TLOC], F32))
        r_sb = es.enter_context(nc.sbuf_tensor("r_sb0", [128, 2, TLOC], F32))
        sq = es.enter_context(nc.sbuf_tensor("sq0", [128, TLOC], F32))
        rrow = es.enter_context(nc.sbuf_tensor("rrow0", [1, 6, TLOC], F32))
        xn = es.enter_context(nc.sbuf_tensor("xn0", [128, 2, TLOC], F32))
        pA = es.enter_context(nc.psum_tensor("pA", [128, TLOC], F32))
        pB = es.enter_context(nc.psum_tensor("pB2", [128, TLOC], F32))
        pS = es.enter_context(nc.psum_tensor("pS", [1, TLOC], F32))
        pQ = es.enter_context(nc.psum_tensor("pQ", [1, TLOC], F32))
        block = es.enter_context(nc.Block())
        C = Ctx(nc, es)
        M = {}

        @block.gpsimd
        def _(gp):
            C.dma("gcst", gp.dma_start(out=x1bf[:], in_=x1T.rearrange("(e p) t -> p e t", p=128)))
            C.dma("gcst", gp.dma_start(out=a2bf[:], in_=loraA2.rearrange("(e p) j -> p e j", p=128)))
            C.dma("gcst", gp.dma_start(out=b2bf[:], in_=loraB2.rearrange("p (o j) -> p o j", j=128)))
            C.dma("gcst", gp.dma_start(out=repb[:], in_=rep16[:]))
        @block.sync
        def _(sp):
            C.dma("cst", sp.dma_start(out=idt[:], in_=ident[:]))
            C.dma("cst", sp.dma_start(out=onc[:], in_=ones_col[:]))
            C.dma("cst", sp.dma_start(out=onr[:], in_=ones_row[:]))
        CST = C.dma_total("cst")
        GCST = C.dma_total("gcst")

        rwb = es.enter_context(nc.sbuf_tensor("rwb", [128, 32, 8], BF16))
        @block.gpsimd
        def _(gp):
            M["rwl"] = C.dma("gcst2", gp.dma_start(out=rwb[:], in_=router_WT.rearrange("(e p) j -> p e j", p=128)))

        @block.tensor
        def _(te):
            C.wait(te, "pe", M["rwl"])
            C.wait(te, "pe", GCST)
            last = None
            for ec in range(32):
                te.ldweights(rwb[:, ec])
                te.matmul(pA[0:8, 0:512], rwb[:, ec], x1bf[:, ec, 0:512], start=(ec == 0), stop=(ec == 31))
                last = te.matmul(pA[0:8, 512:1024], rwb[:, ec], x1bf[:, ec, 512:1024], start=(ec == 0), stop=(ec == 31))
            M["lg_mm"] = C.bump("pe", last)

        @block.scalar
        def _(sc):
            C.wait(sc, "act", M["lg_mm"])
            M["lg_ev"] = C.bump("act", sc.copy(lgT[:], pA[0:8, :]))

        @block.tensor
        def _(te):
            C.wait(te, "pe", M["lg_ev"])
            C.wait(te, "pe", CST)
            last = None
            for ti in range(8):
                last = te.transpose(pB[0:128, 8 * ti:8 * (ti + 1)], lgT[:, 128 * ti:128 * (ti + 1)], idt[0:8, 0:8])
            M["lg_tr"] = C.bump("pe", last)

        @block.scalar
        def _(sc):
            C.wait(sc, "act", M["lg_tr"])
            M["lg2"] = C.bump("act", sc.copy(lg[:].rearrange("p a b -> p (a b)"), pB[0:128, 0:64]))

        for ti in range(8):
            @block.vector
            def _(dv, ti=ti):
                C.wait(dv, "dve", M["lg2"])
                C.self_edge("dve", dv, dv.reduce_max(mrow[:, ti, 0:1], lg[:, ti], axis=AX.X))
                M[f"mx_{ti}"] = C.bump("dve", dv.tensor_scalar(mrow[:, ti, 1:2], mrow[:, ti, 0:1], -1.0, None, OP.mult))
            @block.scalar
            def _(sc, ti=ti):
                C.wait(sc, "act", M[f"mx_{ti}"])
                M[f"e_{ti}"] = C.bump("act", sc.activation(esb[:, ti], lg[:, ti], AF.Exp, bias=mrow[:, ti, 1:2]))
            @block.vector
            def _(dv, ti=ti):
                C.wait(dv, "dve", M[f"e_{ti}"])
                C.self_edge("dve", dv, dv.max(top8[:, ti], esb[:, ti]))
                C.self_edge("dve", dv, dv.tensor_tensor(mrow[:, ti, 2:3], top8[:, ti, 0:1], top8[:, ti, 1:2], OP.add))
                C.self_edge("dve", dv, dv.reciprocal(mrow[:, ti, 3:4], mrow[:, ti, 2:3]))
                C.self_edge("dve", dv, dv.tensor_scalar(wd[:, ti], esb[:, ti], top8[:, ti, 1:2], None, OP.is_ge))
                C.self_edge("dve", dv, dv.tensor_tensor(wd[:, ti], wd[:, ti], esb[:, ti], OP.mult))
                M[f"wd_{ti}"] = C.bump("dve", dv.tensor_scalar(wd[:, ti], wd[:, ti], mrow[:, ti, 3:4], None, OP.mult))

        @block.tensor
        def _(te):
            for ti in range(8):
                C.wait(te, "pe", M[f"wd_{ti}"])
            last = None
            for ti in range(8):
                last = te.transpose(pA[0:8, 128 * ti:128 * (ti + 1)], wd[:, ti], idt[:])
            M["wd_tr"] = C.bump("pe", last)

        @block.scalar
        def _(sc):
            C.wait(sc, "act", M["wd_tr"])
            M["wdT_ev"] = C.bump("act", sc.copy(wdT[:], pA[0:8, :]))

        @block.tensor
        def _(te):
            C.wait(te, "pe", M["wdT_ev"])
            te.ldweights(repb[:])
            te.matmul(pA[0:128, 0:512], repb[:], wdT[:, 0:512], start=True, stop=True)
            M["w16_mm"] = C.bump("pe", te.matmul(pA[0:128, 512:1024], repb[:], wdT[:, 512:1024], start=True, stop=True))

        @block.scalar
        def _(sc):
            C.wait(sc, "act", M["w16_mm"])
            M["w16_ev"] = C.bump("act", sc.copy(w16[:], pA[:]))

        @block.tensor
        def _(te):
            C.wait(te, "pe", M["w16_ev"])
            last = None
            for ec in range(32):
                te.ldweights(a2bf[:, ec])
                te.matmul(pB[:, 0:512], a2bf[:, ec], x1bf[:, ec, 0:512], start=(ec == 0), stop=(ec == 31))
                last = te.matmul(pB[:, 512:1024], a2bf[:, ec], x1bf[:, ec, 512:1024], start=(ec == 0), stop=(ec == 31))
            M["mid_mm"] = C.bump("pe", last)

        @block.vector
        def _(dv):
            C.wait(dv, "dve", M["mid_mm"])
            M["midg"] = C.bump("dve", dv.tensor_tensor(midg[:], pB[:], w16[:], OP.mult))
        if f0dbg:
            @block.sync
            def _(sp):
                C.wait(sp, "sp", M["lg_ev"])
                C.dma("dbg", sp.dma_start(out=lg_out[:], in_=lgT[:]))
                C.wait(sp, "sp", M["wdT_ev"])
                C.dma("dbg", sp.dma_start(out=wd_out[:], in_=wdT[:]))
                C.wait(sp, "sp", M["w16_ev"])
                C.dma("dbg", sp.dma_start(out=w16_out[:], in_=w16[:]))
                C.wait(sp, "sp", M["midg"])
                C.dma("dbg", sp.dma_start(out=midg_out[:], in_=midg[:]))

        for oc in range(32):
            par = oc % 2
            @block.gpsimd
            def _(gp, oc=oc, par=par):
                if oc >= 2:
                    C.wait(gp, "pool", M[f"bmm_{oc-2}"])
                M[f"bwl_{oc}"] = C.dma(f"bwl{par}", gp.dma_start(
                    out=bwbuf[:, par],
                    in_=base_WT[:, 128 * oc:128 * (oc + 1)].rearrange("(e p) j -> p e j", p=128)))

            @block.sync
            def _(sp, oc=oc, par=par):
                if oc >= 2:
                    C.wait(sp, "sp", M[f"resid_{oc-2}"])
                M[f"x1l_{oc}"] = C.dma(f"x1l{par}", sp.dma_start(out=x1f[:, par], in_=x1T[128 * oc:128 * (oc + 1), :]))

            @block.tensor
            def _(te, oc=oc, par=par):
                C.wait(te, "pe", M[f"bwl_{oc}"])
                C.wait(te, "pe", M["midg"])
                if oc >= 1:
                    C.wait(te, "pe", M[f"bev_{oc-1}"])
                    C.wait(te, "pe", M[f"resid_{oc-1}"])
                last = None
                for ec in range(32):
                    te.ldweights(bwbuf[:, par, ec])
                    te.matmul(pA[:, 0:512], bwbuf[:, par, ec], x1bf[:, ec, 0:512], start=(ec == 0), stop=(ec == 31))
                    last = te.matmul(pA[:, 512:1024], bwbuf[:, par, ec], x1bf[:, ec, 512:1024], start=(ec == 0), stop=(ec == 31))
                M[f"bmm_{oc}"] = C.bump("pe", last)
                te.ldweights(b2bf[:, oc])
                te.matmul(pB[:, 0:512], b2bf[:, oc], midg[:, 0:512], start=True, stop=True)
                M[f"imm_{oc}"] = C.bump("pe", te.matmul(pB[:, 512:1024], b2bf[:, oc], midg[:, 512:1024], start=True, stop=True))

            @block.scalar
            def _(sc, oc=oc):
                C.wait(sc, "act", M[f"bmm_{oc}"])
                if oc >= 1:
                    C.wait(sc, "act", M[f"resid_{oc-1}"])
                M[f"bev_{oc}"] = C.bump("act", sc.copy(fb[:], pA[:]))

            @block.vector
            def _(dv, oc=oc, par=par):
                C.wait(dv, "dve", M[f"bev_{oc}"])
                C.wait(dv, "dve", M[f"imm_{oc}"])
                C.wait(dv, "dve", M[f"x1l_{oc}"])
                if oc >= 2:
                    C.wait(dv, "dve", M[f"rst_{oc-2}"])
                dv.tensor_tensor(r_sb[:, par], fb[:], pB[:], OP.add)
                M[f"resid_{oc}"] = C.bump("dve", dv.tensor_tensor(r_sb[:, par], r_sb[:, par], x1f[:, par], OP.add))
            if f0dbg and oc == 0:
                @block.sync
                def _(sp):
                    C.wait(sp, "sp", M["bev_0"])
                    C.dma("dbg", sp.dma_start(out=base_out[:], in_=fb[:]))
                    C.wait(sp, "sp", M["resid_0"])
                    C.dma("dbg", sp.dma_start(out=r_out[:], in_=r_sb[:, 0]))

            @block.scalar
            def _(sc, oc=oc, par=par):
                C.wait(sc, "act", M[f"resid_{oc}"])
                if oc >= 1:
                    C.wait(sc, "act", M[f"stat_{oc-1}"])
                M[f"sq_{oc}"] = C.bump("act", sc.activation(sq[:], r_sb[:, par], AF.Square))

            @block.tensor
            def _(te, oc=oc, par=par):
                C.wait(te, "pe", M[f"sq_{oc}"])
                te.matmul(pS[:, 0:512], onc[:, 0:1], r_sb[:, par, 0:512], start=(oc == 0), stop=(oc == 31))
                te.matmul(pS[:, 512:1024], onc[:, 0:1], r_sb[:, par, 512:1024], start=(oc == 0), stop=(oc == 31))
                te.matmul(pQ[:, 0:512], onc[:, 0:1], sq[:, 0:512], start=(oc == 0), stop=(oc == 31))
                M[f"stat_{oc}"] = C.bump("pe", te.matmul(pQ[:, 512:1024], onc[:, 0:1], sq[:, 512:1024], start=(oc == 0), stop=(oc == 31)))

            @block.sync
            def _(sp, oc=oc, par=par):
                C.wait(sp, "sp", M[f"stat_{oc}"])
                M[f"rst_{oc}"] = C.dma(f"rst{par}", sp.dma_start(out=r_scr[128 * oc:128 * (oc + 1), :], in_=r_sb[:, par]))

        @block.vector
        def _(dv):
            C.wait(dv, "dve", M["stat_31"])
            C.self_edge("dve", dv, dv.tensor_scalar(rrow[:, 0], pS[:], 1.0 / H, None, OP.mult))
            C.self_edge("dve", dv, dv.tensor_scalar(rrow[:, 1], pQ[:], 1.0 / H, None, OP.mult))
            C.self_edge("dve", dv, dv.tensor_tensor(rrow[:, 2], rrow[:, 0], rrow[:, 0], OP.mult))
            C.self_edge("dve", dv, dv.tensor_tensor(rrow[:, 3], rrow[:, 1], rrow[:, 2], OP.subtract))
            M["var"] = C.bump("dve", dv.tensor_scalar(rrow[:, 3], rrow[:, 3], 1.0, 1e-5, OP.mult, OP.add))
        @block.scalar
        def _(sc):
            C.wait(sc, "act", M["var"])
            M["sqrtv"] = C.bump("act", sc.activation(rrow[:, 4], rrow[:, 3], AF.Sqrt))
        @block.vector
        def _(dv):
            C.wait(dv, "dve", M["sqrtv"])
            M["rstd"] = C.bump("dve", dv.reciprocal(rrow[:, 5], rrow[:, 4]))
        @block.tensor
        def _(te):
            C.wait(te, "pe", M["rstd"])
            te.matmul(pB[:, 0:512], onr[:], rrow[:, 5, 0:512], start=True, stop=True)
            te.matmul(pB[:, 512:1024], onr[:], rrow[:, 5, 512:1024], start=True, stop=True)
            te.matmul(pA[:, 0:512], onr[:], rrow[:, 0, 0:512], start=True, stop=True)
            M["bc"] = C.bump("pe", te.matmul(pA[:, 512:1024], onr[:], rrow[:, 0, 512:1024], start=True, stop=True))
        for oc in range(32):
            par = oc % 2
            @block.sync
            def _(sp, oc=oc, par=par):
                C.wait(sp, "sp", C.dma_total("rst0"))
                C.wait(sp, "sp", C.dma_total("rst1"))
                if oc >= 2:
                    C.wait(sp, "sp", M[f"norm_{oc-2}"])
                M[f"rrd_{oc}"] = C.dma(f"rrd{par}", sp.dma_start(out=r_sb[:, par], in_=r_scr[128 * oc:128 * (oc + 1), :]))
            @block.vector
            def _(dv, oc=oc, par=par):
                C.wait(dv, "dve", M[f"rrd_{oc}"])
                C.wait(dv, "dve", M["bc"])
                if oc >= 2:
                    C.wait(dv, "dve", M[f"xst_{oc-2}"])
                dv.tensor_tensor(xn[:, par], r_sb[:, par], pA[:], OP.subtract)
                M[f"norm_{oc}"] = C.bump("dve", dv.tensor_tensor(xn[:, par], xn[:, par], pB[:], OP.mult))
            @block.sync
            def _(sp, oc=oc, par=par):
                C.wait(sp, "sp", M[f"norm_{oc}"])
                M[f"xst_{oc}"] = C.dma(f"xst{par}", sp.dma_start(out=x2_out[128 * oc:128 * (oc + 1), :], in_=xn[:, par]))

        @block.gpsimd
        def _(gp):
            C.wait(gp, "pool", C.dma_total("xst0"))
            C.wait(gp, "pool", C.dma_total("xst1"))
            if f0dbg:
                C.wait(gp, "pool", C.dma_total("dbg"))
    return nc


def stage_ffn0_inputs(d, x1_full):
    rep = np.zeros((8, 128), np.float32)
    for e in range(8):
        rep[e, 16 * e:16 * (e + 1)] = 2.0
    A2 = d["L0_ffn_lora_A"].transpose(1, 0, 2).reshape(H, 128)
    maps = []
    common = dict(
        base_WT=np.ascontiguousarray(d["L0_ffn_base_W"].T),
        router_WT=np.ascontiguousarray(d["L0_ffn_router_W"].T),
        loraA2=np.ascontiguousarray(A2),
        loraB2=np.ascontiguousarray(d["L0_ffn_lora_B"].reshape(128, H)),
        rep16=rep,
        ident=np.eye(128, dtype=np.float32),
        ones_col=np.ones((128, 1), np.float32),
        ones_row=np.ones((1, 128), np.float32),
    )
    for c in range(NCORE):
        cols = [x1_full[:, G * k:G * (k + 1)] for k in groups_of(c)]
        maps.append(dict(common, x1T=np.ascontiguousarray(np.concatenate(cols, axis=1))))
    return maps


def build_qkv():
    nc = bass.Bass("TRN2", target_bir_lowering=False, debug=False, detect_race_conditions=not _os.environ.get("BASS_NO_RD"))
    x2T = nc.dram_tensor("x2T", [H, TLOC], F32, kind="ExternalInput").ap()
    w_qkv = nc.dram_tensor("w_qkv", [H, 4352], F32, kind="ExternalInput").ap()
    qkv_out = nc.dram_tensor("qkv_out", [4352, TLOC], BF16, kind="ExternalOutput").ap()
    es = contextlib.ExitStack()
    with es:
        x2bf = es.enter_context(nc.sbuf_tensor("x2bf", [128, 32, TLOC], BF16))
        wqb = es.enter_context(nc.sbuf_tensor("wqb", [128, 2, 32, 128], BF16))
        ev = es.enter_context(nc.sbuf_tensor("evq", [128, 2, TLOC], BF16))
        pq = es.enter_context(nc.psum_tensor("pq", [128, TLOC], F32))
        block = es.enter_context(nc.Block())
        C = Ctx(nc, es)
        M = {}
        @block.gpsimd
        def _(gp):
            C.dma("gcst", gp.dma_start(out=x2bf[:], in_=x2T.rearrange("(e p) t -> p e t", p=128)))
        GCST = C.dma_total("gcst")
        for fc in range(34):
            par = fc % 2
            @block.gpsimd
            def _(gp, fc=fc, par=par):
                if fc >= 2:
                    C.wait(gp, "pool", M[f"mm_{fc-2}"])
                M[f"wl_{fc}"] = C.dma(f"wl{par}", gp.dma_start(
                    out=wqb[:, par],
                    in_=w_qkv[:, 128 * fc:128 * (fc + 1)].rearrange("(e p) j -> p e j", p=128)))
            @block.tensor
            def _(te, fc=fc, par=par):
                C.wait(te, "pe", M[f"wl_{fc}"])
                if fc == 0:
                    C.wait(te, "pe", GCST)
                if fc >= 1:
                    C.wait(te, "pe", M[f"ev_{fc-1}"])
                last = None
                for ec in range(32):
                    te.ldweights(wqb[:, par, ec])
                    te.matmul(pq[:, 0:512], wqb[:, par, ec], x2bf[:, ec, 0:512], start=(ec == 0), stop=(ec == 31))
                    last = te.matmul(pq[:, 512:1024], wqb[:, par, ec], x2bf[:, ec, 512:1024], start=(ec == 0), stop=(ec == 31))
                M[f"mm_{fc}"] = C.bump("pe", last)
            @block.scalar
            def _(sc, fc=fc, par=par):
                C.wait(sc, "act", M[f"mm_{fc}"])
                if fc >= 2:
                    C.wait(sc, "act", M[f"st_{fc-2}"])
                M[f"ev_{fc}"] = C.bump("act", sc.copy(ev[:, par], pq[:]))
            @block.sync
            def _(sp, fc=fc, par=par):
                C.wait(sp, "sp", M[f"ev_{fc}"])
                M[f"st_{fc}"] = C.dma(f"st{par}", sp.dma_start(out=qkv_out[128 * fc:128 * (fc + 1), :], in_=ev[:, par]))
        @block.gpsimd
        def _(gp):
            C.wait(gp, "pool", C.dma_total("st0"))
            C.wait(gp, "pool", C.dma_total("st1"))
    return nc


def alibi_slopes_py(n=32):
    import math
    cp2 = 2 ** int(math.floor(math.log2(n)))
    base = 2.0 ** (-(2.0 ** (-(math.log2(cp2) - 3))))
    slopes = [base ** i for i in range(1, cp2 + 1)]
    if cp2 != n:
        extra_base = 2.0 ** (-(2.0 ** (-(math.log2(2 * cp2) - 3))))
        rem = min(cp2, n - cp2)
        slopes += [extra_base ** i for i in range(1, 1 + 2 * rem, 2)]
    return slopes


def build_attn():
    nc = bass.Bass("TRN2", target_bir_lowering=False, debug=False, detect_race_conditions=not _os.environ.get("BASS_NO_RD"))
    qTh = nc.dram_tensor("qTh", [512, T], BF16, kind="ExternalInput").ap()
    kT = nc.dram_tensor("kT", [128, T], BF16, kind="ExternalInput").ap()
    vT = nc.dram_tensor("vT", [128, T], BF16, kind="ExternalInput").ap()
    identb = nc.dram_tensor("identb", [128, 128], BF16, kind="ExternalInput").ap()
    kcol = nc.dram_tensor("kcol", [128, 64], F32, kind="ExternalInput").ap()
    maskc = nc.dram_tensor("maskc", [128, 4, 512], F32, kind="ExternalInput").ap()
    onesb = nc.dram_tensor("onesb", [128, 1], BF16, kind="ExternalInput").ap()
    ones_row = nc.dram_tensor("ones_row", [1, 128], F32, kind="ExternalInput").ap()
    slopes_c = nc.dram_tensor("slopes_c", [128, 4], F32, kind="ExternalInput").ap()
    attnT_out = nc.dram_tensor("attnT_out", [512, T], BF16, kind="ExternalOutput").ap()

    DH = 128
    SCL = DH ** -0.5
    NQT = 16
    WS = [64, 256, 512, 512]
    NSUBMAX = 8

    es = contextlib.ExitStack()
    with es:
        qsb = es.enter_context(nc.sbuf_tensor("qsb", [128, 4, T], BF16))
        ksb = es.enter_context(nc.sbuf_tensor("ksb", [128, 64, 128], BF16))
        vtb = es.enter_context(nc.sbuf_tensor("vtb", [128, 64, 128], BF16))
        vsb = es.enter_context(nc.sbuf_tensor("vsb", [128, 64, 128], BF16))
        idb = es.enter_context(nc.sbuf_tensor("idb", [128, 128], BF16))
        kcb = es.enter_context(nc.sbuf_tensor("kcb", [128, 64], F32))
        mkb = es.enter_context(nc.sbuf_tensor("mkb", [128, 4, 512], F32))
        onb = es.enter_context(nc.sbuf_tensor("onb", [128, 1], BF16))
        onr = es.enter_context(nc.sbuf_tensor("onrA", [1, 128], F32))
        slps = es.enter_context(nc.sbuf_tensor("slps", [128, 4], F32))
        slpk = es.enter_context(nc.sbuf_tensor("slpk", [128, 64], F32))
        qtmp = es.enter_context(nc.sbuf_tensor("qtmp", [128, 1], F32))
        bias2 = es.enter_context(nc.sbuf_tensor("bias2", [128, NSUBMAX, 64], F32))
        probs = es.enter_context(nc.sbuf_tensor("probs", [128, 2, 512], BF16))
        rs_row = es.enter_context(nc.sbuf_tensor("rs_row", [1, 2, 512], F32))
        asb = es.enter_context(nc.sbuf_tensor("asb", [128, 512], F32))
        ob = es.enter_context(nc.sbuf_tensor("ob", [128, 2, 512], BF16))
        pqk = es.enter_context(nc.psum_tensor("pqk", [128, 2, 512], F32))
        pattn = es.enter_context(nc.psum_tensor("pattn", [128, 512], F32))
        psums = es.enter_context(nc.psum_tensor("psums", [1, 512], F32))
        pbc = es.enter_context(nc.psum_tensor("pbcA", [128, 512], F32))
        pv = es.enter_context(nc.psum_tensor("pv", [128, 2, 1024], BF16))
        block = es.enter_context(nc.Block())
        C = Ctx(nc, es)
        M = {}

        @block.sync
        def _(sp):
            C.dma("cst", sp.dma_start(out=qsb[:], in_=qTh.rearrange("(hh p) t -> p hh t", p=128)))
            C.dma("cst", sp.dma_start(out=ksb[:], in_=kT.rearrange("p (kt j) -> p kt j", j=128)))
            C.dma("cst", sp.dma_start(out=vtb[:], in_=vT.rearrange("p (kt j) -> p kt j", j=128)))
            C.dma("cst", sp.dma_start(out=idb[:], in_=identb[:]))
            C.dma("cst", sp.dma_start(out=kcb[:], in_=kcol[:]))
            C.dma("cst", sp.dma_start(out=mkb[:], in_=maskc[:]))
            C.dma("cst", sp.dma_start(out=onb[:], in_=onesb[:]))
            C.dma("cst", sp.dma_start(out=onr[:], in_=ones_row[:]))
            C.dma("cst", sp.dma_start(out=slps[:], in_=slopes_c[:]))
        CST = C.dma_total("cst")

        for kt in range(64):
            par = kt % 2
            @block.tensor
            def _(te, kt=kt, par=par):
                C.wait(te, "pe", CST)
                if kt >= 2:
                    C.wait(te, "pe", M[f"vev_{kt-2}"])
                M[f"vtr_{kt}"] = C.bump("pe", te.transpose(pv[:, par, 0:128], vtb[:, kt], idb[:]))
            @block.scalar
            def _(sc, kt=kt, par=par):
                C.wait(sc, "act", M[f"vtr_{kt}"])
                M[f"vev_{kt}"] = C.bump("act", sc.copy(vsb[:, kt], pv[:, par, 0:128]))

        probs_free = {}
        pqk_free = {}
        for hh in range(4):
            W = WS[hh]
            NSUB = 512 // W
            @block.vector
            def _(dv, hh=hh):
                C.wait(dv, "dve", CST)
                if hh >= 1:
                    C.wait(dv, "dve", M["bias_used"])
                M[f"slpk_{hh}"] = C.self_edge("dve", dv, dv.tensor_scalar(slpk[:], kcb[:], slps[:, hh:hh + 1], None, OP.mult))

            for qt in range(NQT):
                q0 = 512 * qt
                nk = q0 // 128 + 4
                @block.vector
                def _(dv, hh=hh, qt=qt, q0=q0, nk=nk, W=W, NSUB=NSUB):
                    C.wait(dv, "dve", M[f"slpk_{hh}"])
                    if (hh, qt) != (0, 0):
                        C.wait(dv, "dve", M["bias_used2"])
                    last = None
                    for sub in range(NSUB):
                        qref = float(q0 + (sub + 1) * W - 1)
                        C.self_edge("dve", dv, dv.tensor_scalar(qtmp[:], slps[:, hh:hh + 1], qref, None, OP.mult))
                        last = dv.tensor_scalar(bias2[:, sub, 0:nk], slpk[:, 0:nk], qtmp[:, 0:1], None, OP.subtract)
                    M[f"bias_{hh}_{qt}"] = C.self_edge("dve", dv, last)
                M["bias_used"] = M[f"bias_{hh}_{qt}"]

                for kt in range(nk):
                    par = kt % 2
                    diag = kt - (nk - 4)
                    @block.tensor
                    def _(te, hh=hh, qt=qt, q0=q0, kt=kt, par=par):
                        if par in pqk_free:
                            C.wait(te, "pe", pqk_free[par])
                        te.ldweights(ksb[:, kt])
                        M[f"qk_{par}"] = C.bump("pe", te.matmul(pqk[:, par], ksb[:, kt], qsb[:, hh, q0:q0 + 512], start=True, stop=True))

                    if diag >= 0:
                        @block.vector
                        def _(dv, kt=kt, par=par, diag=diag):
                            C.wait(dv, "dve", M[f"qk_{par}"])
                            M[f"qk_{par}"] = C.bump("dve", dv.tensor_tensor(pqk[:, par], pqk[:, par], mkb[:, diag], OP.add))

                    @block.scalar
                    def _(sc, hh=hh, qt=qt, kt=kt, par=par, W=W, NSUB=NSUB):
                        C.wait(sc, "act", M[f"qk_{par}"])
                        C.wait(sc, "act", M[f"bias_{hh}_{qt}"])
                        if par in probs_free:
                            C.wait(sc, "act", probs_free[par])
                        last = None
                        for sub in range(NSUB):
                            last = sc.activation(probs[:, par, W * sub:W * (sub + 1)], pqk[:, par, W * sub:W * (sub + 1)],
                                                 AF.Exp, scale=SCL, bias=bias2[:, sub, kt:kt + 1])
                        M[f"pr_{par}"] = C.bump("act", last)
                        pqk_free[par] = M[f"pr_{par}"]

                    @block.tensor
                    def _(te, hh=hh, qt=qt, kt=kt, par=par, nk=nk):
                        C.wait(te, "pe", M[f"pr_{par}"])
                        if kt == 0:
                            if "fin" in M:
                                C.wait(te, "pe", M["fin"])
                            if "aev" in M:
                                C.wait(te, "pe", M["aev"])
                            if "rec" in M:
                                C.wait(te, "pe", M["rec"])
                        st = (kt == 0)
                        sto = (kt == nk - 1)
                        te.ldweights(vsb[:, kt])
                        te.matmul(pattn[:], vsb[:, kt], probs[:, par], start=st, stop=sto)
                        te.ldweights(onb[:])
                        M[f"pv_{par}"] = C.bump("pe", te.matmul(psums[:], onb[:], probs[:, par], start=st, stop=sto))
                        probs_free[par] = M[f"pv_{par}"]
                        if sto:
                            M["pv_done"] = M[f"pv_{par}"]

                @block.vector
                def _(dv, hh=hh, qt=qt):
                    C.wait(dv, "dve", M["pv_done"])
                    C.self_edge("dve", dv, dv.tensor_scalar(rs_row[:, 0], psums[:], 1e-35, None, OP.max))
                    M["rec"] = C.self_edge("dve", dv, dv.reciprocal(rs_row[:, 1], rs_row[:, 0]))

                @block.scalar
                def _(sc, hh=hh, qt=qt):
                    C.wait(sc, "act", M["pv_done"])
                    if "fin" in M:
                        C.wait(sc, "act", M["fin"])
                    M["aev"] = C.bump("act", sc.copy(asb[:], pattn[:]))

                @block.tensor
                def _(te, hh=hh, qt=qt):
                    C.wait(te, "pe", M["rec"])
                    if "fin" in M:
                        C.wait(te, "pe", M["fin"])
                    M["bcr"] = C.bump("pe", te.matmul(pbc[:], onr[:], rs_row[:, 1], start=True, stop=True))

                opar = (4 * 16 + qt) % 2 if False else (hh * NQT + qt) % 2
                @block.vector
                def _(dv, hh=hh, qt=qt, opar=opar):
                    C.wait(dv, "dve", M["bcr"])
                    C.wait(dv, "dve", M["aev"])
                    if f"ost_{opar}" in M:
                        C.wait(dv, "dve", M[f"ost_{opar}"])
                    M["fin"] = C.bump("dve", dv.tensor_tensor(ob[:, opar], asb[:], pbc[:], OP.mult))

                @block.sync
                def _(sp, hh=hh, qt=qt, q0=q0, opar=opar):
                    C.wait(sp, "sp", M["fin"])
                    M[f"ost_{opar}"] = C.dma(f"ost{opar}", sp.dma_start(
                        out=attnT_out[128 * hh:128 * (hh + 1), q0:q0 + 512], in_=ob[:, opar]))
                M["bias_used2"] = M[f"pr_{(nk - 1) % 2}"]

        @block.gpsimd
        def _(gp):
            C.wait(gp, "pool", C.dma_total("ost0"))
            C.wait(gp, "pool", C.dma_total("ost1"))
    return nc


def build_wc():
    nc = bass.Bass("TRN2", target_bir_lowering=False, debug=False, detect_race_conditions=not _os.environ.get("BASS_NO_RD"))
    attnT = nc.dram_tensor("attnT", [H, TLOC], BF16, kind="ExternalInput").ap()
    w_c = nc.dram_tensor("w_c", [H, H], F32, kind="ExternalInput").ap()
    x2T = nc.dram_tensor("x2T", [H, TLOC], F32, kind="ExternalInput").ap()
    ones_col = nc.dram_tensor("ones_col", [128, 1], F32, kind="ExternalInput").ap()
    ones_row = nc.dram_tensor("ones_row", [1, 128], F32, kind="ExternalInput").ap()
    x3_out = nc.dram_tensor("x3_out", [H, TLOC], F32, kind="ExternalOutput").ap()
    r_scr = nc.dram_tensor("r_scrW", [H, TLOC], F32, kind="Internal").ap()

    es = contextlib.ExitStack()
    with es:
        abf = es.enter_context(nc.sbuf_tensor("abf", [128, 32, TLOC], BF16))
        wbuf = es.enter_context(nc.sbuf_tensor("wbufW", [128, 2, 32, 128], BF16))
        x2t = es.enter_context(nc.sbuf_tensor("x2tW", [128, 2, TLOC], F32))
        r_sb = es.enter_context(nc.sbuf_tensor("r_sbW", [128, 2, TLOC], F32))
        sq = es.enter_context(nc.sbuf_tensor("sqW", [128, TLOC], F32))
        rrow = es.enter_context(nc.sbuf_tensor("rrowW", [1, 6, TLOC], F32))
        onc = es.enter_context(nc.sbuf_tensor("oncW", [128, 1], F32))
        onr = es.enter_context(nc.sbuf_tensor("onrW", [1, 128], F32))
        xn = es.enter_context(nc.sbuf_tensor("xnW", [128, 2, TLOC], F32))
        pW = es.enter_context(nc.psum_tensor("pWW", [128, TLOC], F32))
        psum_s = es.enter_context(nc.psum_tensor("pSW", [1, TLOC], F32))
        psum_q = es.enter_context(nc.psum_tensor("pQW", [1, TLOC], F32))
        pbc = es.enter_context(nc.psum_tensor("pbcW", [128, TLOC], F32))
        block = es.enter_context(nc.Block())
        C = Ctx(nc, es)
        M = {}

        @block.sync
        def _(sp):
            C.dma("cst", sp.dma_start(out=onc[:], in_=ones_col[:]))
            C.dma("cst", sp.dma_start(out=onr[:], in_=ones_row[:]))
        CST = C.dma_total("cst")
        @block.gpsimd
        def _(gp):
            C.dma("gcst", gp.dma_start(out=abf[:], in_=attnT.rearrange("(e p) t -> p e t", p=128)))
        GCST = C.dma_total("gcst")

        for hc in range(32):
            par = hc % 2
            @block.gpsimd
            def _(gp, hc=hc, par=par):
                if hc >= 2:
                    C.wait(gp, "pool", M[f"mm_{hc-2}"])
                M[f"wl_{hc}"] = C.dma(f"wl{par}", gp.dma_start(
                    out=wbuf[:, par],
                    in_=w_c[:, 128 * hc:128 * (hc + 1)].rearrange("(e p) j -> p e j", p=128)))
            @block.sync
            def _(sp, hc=hc, par=par):
                if hc >= 2:
                    C.wait(sp, "sp", M[f"resid_{hc-2}"])
                M[f"x2l_{hc}"] = C.dma(f"x2l{par}", sp.dma_start(out=x2t[:, par], in_=x2T[128 * hc:128 * (hc + 1), :]))
            @block.tensor
            def _(te, hc=hc, par=par):
                C.wait(te, "pe", M[f"wl_{hc}"])
                if hc == 0:
                    C.wait(te, "pe", GCST)
                if hc >= 1:
                    C.wait(te, "pe", M[f"resid_{hc-1}"])
                last = None
                for ec in range(32):
                    te.ldweights(wbuf[:, par, ec])
                    te.matmul(pW[:, 0:512], wbuf[:, par, ec], abf[:, ec, 0:512], start=(ec == 0), stop=(ec == 31))
                    last = te.matmul(pW[:, 512:1024], wbuf[:, par, ec], abf[:, ec, 512:1024], start=(ec == 0), stop=(ec == 31))
                M[f"mm_{hc}"] = C.bump("pe", last)
            @block.vector
            def _(dv, hc=hc, par=par):
                C.wait(dv, "dve", M[f"mm_{hc}"])
                C.wait(dv, "dve", M[f"x2l_{hc}"])
                if hc >= 2:
                    C.wait(dv, "dve", M[f"rst_{hc-2}"])
                M[f"resid_{hc}"] = C.bump("dve", dv.tensor_tensor(r_sb[:, par], pW[:], x2t[:, par], OP.add))
            @block.scalar
            def _(sc, hc=hc, par=par):
                C.wait(sc, "act", M[f"resid_{hc}"])
                if hc >= 1:
                    C.wait(sc, "act", M[f"stat_{hc-1}"])
                M[f"sq_{hc}"] = C.bump("act", sc.activation(sq[:], r_sb[:, par], AF.Square))
            @block.tensor
            def _(te, hc=hc, par=par):
                C.wait(te, "pe", M[f"sq_{hc}"])
                C.wait(te, "pe", CST)
                te.matmul(psum_s[:, 0:512], onc[:, 0:1], r_sb[:, par, 0:512], start=(hc == 0), stop=(hc == 31))
                te.matmul(psum_s[:, 512:1024], onc[:, 0:1], r_sb[:, par, 512:1024], start=(hc == 0), stop=(hc == 31))
                te.matmul(psum_q[:, 0:512], onc[:, 0:1], sq[:, 0:512], start=(hc == 0), stop=(hc == 31))
                M[f"stat_{hc}"] = C.bump("pe", te.matmul(psum_q[:, 512:1024], onc[:, 0:1], sq[:, 512:1024], start=(hc == 0), stop=(hc == 31)))
            @block.sync
            def _(sp, hc=hc, par=par):
                C.wait(sp, "sp", M[f"stat_{hc}"])
                M[f"rst_{hc}"] = C.dma(f"rst{par}", sp.dma_start(out=r_scr[128 * hc:128 * (hc + 1), :], in_=r_sb[:, par]))

        @block.vector
        def _(dv):
            C.wait(dv, "dve", M["stat_31"])
            C.self_edge("dve", dv, dv.tensor_scalar(rrow[:, 0], psum_s[:], 1.0 / H, None, OP.mult))
            C.self_edge("dve", dv, dv.tensor_scalar(rrow[:, 1], psum_q[:], 1.0 / H, None, OP.mult))
            C.self_edge("dve", dv, dv.tensor_tensor(rrow[:, 2], rrow[:, 0], rrow[:, 0], OP.mult))
            C.self_edge("dve", dv, dv.tensor_tensor(rrow[:, 3], rrow[:, 1], rrow[:, 2], OP.subtract))
            M["var"] = C.bump("dve", dv.tensor_scalar(rrow[:, 3], rrow[:, 3], 1.0, 1e-5, OP.mult, OP.add))
        @block.scalar
        def _(sc):
            C.wait(sc, "act", M["var"])
            M["sqrtv"] = C.bump("act", sc.activation(rrow[:, 4], rrow[:, 3], AF.Sqrt))
        @block.vector
        def _(dv):
            C.wait(dv, "dve", M["sqrtv"])
            M["rstd"] = C.self_edge("dve", dv, dv.reciprocal(rrow[:, 5], rrow[:, 4]))
        @block.tensor
        def _(te):
            C.wait(te, "pe", M["rstd"])
            te.matmul(pbc[:, 0:512], onr[:], rrow[:, 5, 0:512], start=True, stop=True)
            te.matmul(pbc[:, 512:1024], onr[:], rrow[:, 5, 512:1024], start=True, stop=True)
            te.matmul(pW[:, 0:512], onr[:], rrow[:, 0, 0:512], start=True, stop=True)
            M["bc"] = C.bump("pe", te.matmul(pW[:, 512:1024], onr[:], rrow[:, 0, 512:1024], start=True, stop=True))
        for hc in range(32):
            par = hc % 2
            @block.sync
            def _(sp, hc=hc, par=par):
                C.wait(sp, "sp", C.dma_total("rst0"))
                C.wait(sp, "sp", C.dma_total("rst1"))
                if hc >= 2:
                    C.wait(sp, "sp", M[f"norm_{hc-2}"])
                M[f"rrd_{hc}"] = C.dma(f"rrd{par}", sp.dma_start(out=r_sb[:, par], in_=r_scr[128 * hc:128 * (hc + 1), :]))
            @block.vector
            def _(dv, hc=hc, par=par):
                C.wait(dv, "dve", M[f"rrd_{hc}"])
                C.wait(dv, "dve", M["bc"])
                if hc >= 2:
                    C.wait(dv, "dve", M[f"xst_{hc-2}"])
                dv.tensor_tensor(xn[:, par], r_sb[:, par], pW[:], OP.subtract)
                M[f"norm_{hc}"] = C.bump("dve", dv.tensor_tensor(xn[:, par], xn[:, par], pbc[:], OP.mult))
            @block.sync
            def _(sp, hc=hc, par=par):
                C.wait(sp, "sp", M[f"norm_{hc}"])
                M[f"xst_{hc}"] = C.dma(f"xst{par}", sp.dma_start(out=x3_out[128 * hc:128 * (hc + 1), :], in_=xn[:, par]))
        @block.gpsimd
        def _(gp):
            C.wait(gp, "pool", C.dma_total("xst0"))
            C.wait(gp, "pool", C.dma_total("xst1"))
    return nc


def build_ffn1():
    nc = bass.Bass("TRN2", target_bir_lowering=False, debug=False, detect_race_conditions=not _os.environ.get("BASS_NO_RD"))
    x3T = nc.dram_tensor("x3T", [H, TLOC], F32, kind="ExternalInput").ap()
    rwT1 = nc.dram_tensor("rwT1", [H, 64], F32, kind="ExternalInput").ap()
    rb1 = nc.dram_tensor("rb1", [64, 1], F32, kind="ExternalInput").ap()
    ebias = nc.dram_tensor("ebias", [64, H], F32, kind="ExternalInput").ap()
    ident = nc.dram_tensor("identF", [128, 128], F32, kind="ExternalInput").ap()
    ones_col = nc.dram_tensor("ones_col", [128, 1], F32, kind="ExternalInput").ap()
    ones_row = nc.dram_tensor("ones_row", [1, 128], F32, kind="ExternalInput").ap()
    onesrb = nc.dram_tensor("onesrb", [1, 128], BF16, kind="ExternalInput").ap()
    x4_out = nc.dram_tensor("x4_out", [H, TLOC], F32, kind="ExternalOutput").ap()
    r_scr = nc.dram_tensor("r_scrF", [H, TLOC], F32, kind="Internal").ap()

    es = contextlib.ExitStack()
    with es:
        x3bf = es.enter_context(nc.sbuf_tensor("x3bf", [128, 32, TLOC], BF16))
        rwb = es.enter_context(nc.sbuf_tensor("rwb1", [128, 32, 64], BF16))
        rbb = es.enter_context(nc.sbuf_tensor("rbb", [64, 1], F32))
        ebb = es.enter_context(nc.sbuf_tensor("ebb", [64, 32, 128], BF16))
        idt = es.enter_context(nc.sbuf_tensor("idtF", [128, 128], F32))
        onc = es.enter_context(nc.sbuf_tensor("oncF", [128, 1], F32))
        onr = es.enter_context(nc.sbuf_tensor("onrF", [1, 128], F32))
        onrb = es.enter_context(nc.sbuf_tensor("onrbF", [1, 128], BF16))
        lgT = es.enter_context(nc.sbuf_tensor("lgTF", [64, TLOC], F32))
        lg = es.enter_context(nc.sbuf_tensor("lgF", [128, 8, 64], F32))
        esb = es.enter_context(nc.sbuf_tensor("esbF", [128, 8, 64], F32))
        wdd = es.enter_context(nc.sbuf_tensor("wdd", [128, 8, 65], F32))
        top8 = es.enter_context(nc.sbuf_tensor("top8F", [128, 8, 8], F32))
        mrow = es.enter_context(nc.sbuf_tensor("mrowF", [128, 8, 4], F32))
        wdT = es.enter_context(nc.sbuf_tensor("wdTF", [65, TLOC], BF16))
        wsrow = es.enter_context(nc.sbuf_tensor("wsrow", [1, TLOC], BF16))
        msb = es.enter_context(nc.sbuf_tensor("msbF", [128, TLOC], F32))
        x3f = es.enter_context(nc.sbuf_tensor("x3fF", [128, 2, TLOC], F32))
        r_sb = es.enter_context(nc.sbuf_tensor("r_sbF", [128, 2, TLOC], F32))
        sq = es.enter_context(nc.sbuf_tensor("sqF", [128, TLOC], F32))
        rrow = es.enter_context(nc.sbuf_tensor("rrowF", [1, 6, TLOC], F32))
        xn = es.enter_context(nc.sbuf_tensor("xnF", [128, 2, TLOC], F32))
        pA = es.enter_context(nc.psum_tensor("pAF", [128, TLOC], F32))
        pB = es.enter_context(nc.psum_tensor("pBF", [128, TLOC], F32))
        pS = es.enter_context(nc.psum_tensor("pSF", [1, TLOC], F32))
        pQ = es.enter_context(nc.psum_tensor("pQF", [1, TLOC], F32))
        block = es.enter_context(nc.Block())
        C = Ctx(nc, es)
        M = {}

        @block.gpsimd
        def _(gp):
            C.dma("gcst", gp.dma_start(out=x3bf[:], in_=x3T.rearrange("(e p) t -> p e t", p=128)))
            C.dma("gcst", gp.dma_start(out=rwb[:], in_=rwT1.rearrange("(e p) j -> p e j", p=128)))
            C.dma("gcst", gp.dma_start(out=ebb[:], in_=ebias.rearrange("p (o j) -> p o j", j=128)))
        @block.sync
        def _(sp):
            C.dma("cst", sp.dma_start(out=rbb[:], in_=rb1[:]))
            C.dma("cst", sp.dma_start(out=idt[:], in_=ident[:]))
            C.dma("cst", sp.dma_start(out=onc[:], in_=ones_col[:]))
            C.dma("cst", sp.dma_start(out=onr[:], in_=ones_row[:]))
            C.dma("cst", sp.dma_start(out=onrb[:], in_=onesrb[:]))
        CST = C.dma_total("cst")
        GCST = C.dma_total("gcst")

        @block.tensor
        def _(te):
            C.wait(te, "pe", GCST)
            last = None
            for ec in range(32):
                te.ldweights(rwb[:, ec])
                te.matmul(pA[0:64, 0:512], rwb[:, ec], x3bf[:, ec, 0:512], start=(ec == 0), stop=(ec == 31))
                last = te.matmul(pA[0:64, 512:1024], rwb[:, ec], x3bf[:, ec, 512:1024], start=(ec == 0), stop=(ec == 31))
            M["lg_mm"] = C.bump("pe", last)
        @block.vector
        def _(dv):
            C.wait(dv, "dve", M["lg_mm"])
            C.wait(dv, "dve", CST)
            M["lg_ev"] = C.bump("dve", dv.tensor_scalar(lgT[:], pA[0:64, :], rbb[:, 0:1], None, OP.add))
        @block.tensor
        def _(te):
            C.wait(te, "pe", M["lg_ev"])
            last = None
            for ti in range(8):
                last = te.transpose(pB[0:128, 64 * ti:64 * (ti + 1)], lgT[:, 128 * ti:128 * (ti + 1)], idt[0:64, 0:64])
            M["lg_tr"] = C.bump("pe", last)
        @block.scalar
        def _(sc):
            C.wait(sc, "act", M["lg_tr"])
            M["lg2"] = C.bump("act", sc.copy(lg[:].rearrange("p a b -> p (a b)"), pB[0:128, 0:512]))

        for ti in range(8):
            @block.vector
            def _(dv, ti=ti):
                C.wait(dv, "dve", M["lg2"])
                C.self_edge("dve", dv, dv.reduce_max(mrow[:, ti, 0:1], lg[:, ti], axis=AX.X))
                M[f"mx_{ti}"] = C.bump("dve", dv.tensor_scalar(mrow[:, ti, 1:2], mrow[:, ti, 0:1], -1.0, None, OP.mult))
            @block.scalar
            def _(sc, ti=ti):
                C.wait(sc, "act", M[f"mx_{ti}"])
                M[f"e_{ti}"] = C.bump("act", sc.activation(esb[:, ti], lg[:, ti], AF.Exp, bias=mrow[:, ti, 1:2]))
            @block.vector
            def _(dv, ti=ti):
                C.wait(dv, "dve", M[f"e_{ti}"])
                C.self_edge("dve", dv, dv.reduce_sum(mrow[:, ti, 2:3], esb[:, ti], axis=AX.X))
                C.self_edge("dve", dv, dv.reciprocal(mrow[:, ti, 3:4], mrow[:, ti, 2:3]))
                C.self_edge("dve", dv, dv.tensor_scalar(esb[:, ti], esb[:, ti], mrow[:, ti, 3:4], None, OP.mult))
                C.self_edge("dve", dv, dv.max(top8[:, ti], esb[:, ti]))
                C.self_edge("dve", dv, dv.reduce_sum(wdd[:, ti, 64:65], top8[:, ti], axis=AX.X))
                C.self_edge("dve", dv, dv.tensor_scalar(wdd[:, ti, 0:64], esb[:, ti], top8[:, ti, 7:8], None, OP.is_ge))
                M[f"wd_{ti}"] = C.bump("dve", dv.tensor_tensor(wdd[:, ti, 0:64], wdd[:, ti, 0:64], esb[:, ti], OP.mult))

        @block.tensor
        def _(te):
            for ti in range(8):
                C.wait(te, "pe", M[f"wd_{ti}"])
            last = None
            for ti in range(8):
                last = te.transpose(pA[0:65, 128 * ti:128 * (ti + 1)], wdd[:, ti], idt[:])
            M["wd_tr"] = C.bump("pe", last)
        @block.scalar
        def _(sc):
            C.wait(sc, "act", M["wd_tr"])
            sc.copy(wdT[:], pA[0:65, :])
            M["wdT_ev"] = C.bump("act", sc.copy(wsrow[:], pA[64:65, :]))

        for hc in range(32):
            par = hc % 2
            @block.sync
            def _(sp, hc=hc, par=par):
                if hc >= 2:
                    C.wait(sp, "sp", M[f"resid_{hc-2}"])
                M[f"x3l_{hc}"] = C.dma(f"x3l{par}", sp.dma_start(out=x3f[:, par], in_=x3T[128 * hc:128 * (hc + 1), :]))
            @block.tensor
            def _(te, hc=hc, par=par):
                C.wait(te, "pe", M["wdT_ev"])
                if hc >= 1:
                    C.wait(te, "pe", M[f"mev_{hc-1}"])
                    C.wait(te, "pe", M[f"resid_{hc-1}"])
                te.ldweights(ebb[:, hc])
                te.matmul(pB[:, 0:512], ebb[:, hc], wdT[0:64, 0:512], start=True, stop=True)
                M[f"mix_{hc}"] = C.bump("pe", te.matmul(pB[:, 512:1024], ebb[:, hc], wdT[0:64, 512:1024], start=True, stop=True))
                te.ldweights(onrb[:])
                te.matmul(pA[:, 0:512], onrb[:], wsrow[:, 0:512], start=True, stop=True)
                M[f"ws_{hc}"] = C.bump("pe", te.matmul(pA[:, 512:1024], onrb[:], wsrow[:, 512:1024], start=True, stop=True))
            @block.scalar
            def _(sc, hc=hc):
                C.wait(sc, "act", M[f"mix_{hc}"])
                if hc >= 1:
                    C.wait(sc, "act", M[f"resid_{hc-1}"])
                M[f"mev_{hc}"] = C.bump("act", sc.copy(msb[:], pB[:]))
            @block.vector
            def _(dv, hc=hc, par=par):
                C.wait(dv, "dve", M[f"ws_{hc}"])
                C.wait(dv, "dve", M[f"mev_{hc}"])
                C.wait(dv, "dve", M[f"x3l_{hc}"])
                if hc >= 2:
                    C.wait(dv, "dve", M[f"rst_{hc-2}"])
                dv.tensor_tensor(r_sb[:, par], x3f[:, par], pA[:], OP.mult)
                dv.tensor_tensor(r_sb[:, par], r_sb[:, par], msb[:], OP.add)
                M[f"resid_{hc}"] = C.bump("dve", dv.tensor_tensor(r_sb[:, par], r_sb[:, par], x3f[:, par], OP.add))
            @block.scalar
            def _(sc, hc=hc, par=par):
                C.wait(sc, "act", M[f"resid_{hc}"])
                if hc >= 1:
                    C.wait(sc, "act", M[f"stat_{hc-1}"])
                M[f"sq_{hc}"] = C.bump("act", sc.activation(sq[:], r_sb[:, par], AF.Square))
            @block.tensor
            def _(te, hc=hc, par=par):
                C.wait(te, "pe", M[f"sq_{hc}"])
                te.matmul(pS[:, 0:512], onc[:, 0:1], r_sb[:, par, 0:512], start=(hc == 0), stop=(hc == 31))
                te.matmul(pS[:, 512:1024], onc[:, 0:1], r_sb[:, par, 512:1024], start=(hc == 0), stop=(hc == 31))
                te.matmul(pQ[:, 0:512], onc[:, 0:1], sq[:, 0:512], start=(hc == 0), stop=(hc == 31))
                M[f"stat_{hc}"] = C.bump("pe", te.matmul(pQ[:, 512:1024], onc[:, 0:1], sq[:, 512:1024], start=(hc == 0), stop=(hc == 31)))
            @block.sync
            def _(sp, hc=hc, par=par):
                C.wait(sp, "sp", M[f"stat_{hc}"])
                M[f"rst_{hc}"] = C.dma(f"rst{par}", sp.dma_start(out=r_scr[128 * hc:128 * (hc + 1), :], in_=r_sb[:, par]))

        @block.vector
        def _(dv):
            C.wait(dv, "dve", M["stat_31"])
            C.self_edge("dve", dv, dv.tensor_scalar(rrow[:, 0], pS[:], 1.0 / H, None, OP.mult))
            C.self_edge("dve", dv, dv.tensor_scalar(rrow[:, 1], pQ[:], 1.0 / H, None, OP.mult))
            C.self_edge("dve", dv, dv.tensor_tensor(rrow[:, 2], rrow[:, 0], rrow[:, 0], OP.mult))
            C.self_edge("dve", dv, dv.tensor_tensor(rrow[:, 3], rrow[:, 1], rrow[:, 2], OP.subtract))
            M["var"] = C.bump("dve", dv.tensor_scalar(rrow[:, 3], rrow[:, 3], 1.0, 1e-5, OP.mult, OP.add))
        @block.scalar
        def _(sc):
            C.wait(sc, "act", M["var"])
            M["sqrtv"] = C.bump("act", sc.activation(rrow[:, 4], rrow[:, 3], AF.Sqrt))
        @block.vector
        def _(dv):
            C.wait(dv, "dve", M["sqrtv"])
            M["rstd"] = C.self_edge("dve", dv, dv.reciprocal(rrow[:, 5], rrow[:, 4]))
        @block.tensor
        def _(te):
            C.wait(te, "pe", M["rstd"])
            te.matmul(pB[:, 0:512], onr[:], rrow[:, 5, 0:512], start=True, stop=True)
            te.matmul(pB[:, 512:1024], onr[:], rrow[:, 5, 512:1024], start=True, stop=True)
            te.matmul(pA[:, 0:512], onr[:], rrow[:, 0, 0:512], start=True, stop=True)
            M["bc"] = C.bump("pe", te.matmul(pA[:, 512:1024], onr[:], rrow[:, 0, 512:1024], start=True, stop=True))
        for hc in range(32):
            par = hc % 2
            @block.sync
            def _(sp, hc=hc, par=par):
                C.wait(sp, "sp", C.dma_total("rst0"))
                C.wait(sp, "sp", C.dma_total("rst1"))
                if hc >= 2:
                    C.wait(sp, "sp", M[f"norm_{hc-2}"])
                M[f"rrd_{hc}"] = C.dma(f"rrd{par}", sp.dma_start(out=r_sb[:, par], in_=r_scr[128 * hc:128 * (hc + 1), :]))
            @block.vector
            def _(dv, hc=hc, par=par):
                C.wait(dv, "dve", M[f"rrd_{hc}"])
                C.wait(dv, "dve", M["bc"])
                if hc >= 2:
                    C.wait(dv, "dve", M[f"xst_{hc-2}"])
                dv.tensor_tensor(xn[:, par], r_sb[:, par], pA[:], OP.subtract)
                M[f"norm_{hc}"] = C.bump("dve", dv.tensor_tensor(xn[:, par], xn[:, par], pB[:], OP.mult))
            @block.sync
            def _(sp, hc=hc, par=par):
                C.wait(sp, "sp", M[f"norm_{hc}"])
                M[f"xst_{hc}"] = C.dma(f"xst{par}", sp.dma_start(out=x4_out[128 * hc:128 * (hc + 1), :], in_=xn[:, par]))
        @block.gpsimd
        def _(gp):
            C.wait(gp, "pool", C.dma_total("xst0"))
            C.wait(gp, "pool", C.dma_total("xst1"))
    return nc


def stage_attn_chain(d, x2_full, qkv_cores):
    qkv_full = gather_tokens([qkv_cores[c] for c in range(NCORE)])
    kT = np.ascontiguousarray(qkv_full[4096:4224])
    vT = np.ascontiguousarray(qkv_full[4224:4352])
    maskc = np.zeros((128, 4, 512), np.float32)
    for i in range(4):
        for p in range(128):
            maskc[p, i, 0:min(128 * i + p, 512)] = 0.0
            if 128 * i + p > 0:
                maskc[p, i, 0:min(128 * i + p, 512)] = -1e30
    maskc = np.where((128 * np.arange(4)[None, :, None] + np.arange(128)[:, None, None]) > np.arange(512)[None, None, :],
                     np.float32(-1e30), np.float32(0.0)).astype(np.float32)
    kcolv = (128 * np.arange(64)[None, :] + np.arange(128)[:, None]).astype(np.float32)
    slopes = np.asarray(alibi_slopes_py(32), np.float32)
    maps = []
    for c in range(NCORE):
        heads = [c, 8 + c, 16 + c, 24 + c]
        qTh = np.ascontiguousarray(np.concatenate([qkv_full[128 * h:128 * (h + 1)] for h in heads], axis=0))
        sl = np.broadcast_to(slopes[heads][None, :], (128, 4))
        maps.append(dict(
            qTh=qTh, kT=kT, vT=vT,
            identb=np.eye(128, dtype=np.float32).astype(bf16),
            kcol=kcolv, maskc=maskc,
            onesb=np.ones((128, 1), np.float32).astype(bf16),
            ones_row=np.ones((1, 128), np.float32),
            slopes_c=np.ascontiguousarray(sl.astype(np.float32)),
        ))
    return maps


_PROGS = {}

def _prog(name, builder):
    if name not in _PROGS:
        _PROGS[name] = builder()
    return _PROGS[name]


def kernel(**inputs):
    d = {k: np.ascontiguousarray(np.asarray(v)) for k, v in inputs.items()}

    nc = _prog("m1", build_m1)
    maps = stage_m1_inputs(d)
    res = run_bass_kernel_spmd(nc, maps, core_ids=list(range(NCORE)))
    m1r = [{k: res.results[c][k].view(bf16) for k in ("v_out", "gate_out", "dtn_out")} for c in range(NCORE)]

    nc2 = _prog("m2", build_m2)
    maps2 = stage_m2_inputs(d, m1r)
    res2 = run_bass_kernel_spmd(nc2, maps2, core_ids=list(range(NCORE)))
    y2_cores = [res2.results[c]["y2_out"].view(bf16) for c in range(NCORE)]

    nc3 = _prog("m3", build_m3)
    maps3 = stage_m3_inputs(d, m1r, y2_cores)
    res3 = run_bass_kernel_spmd(nc3, maps3, core_ids=list(range(NCORE)))
    x1_full = gather_tokens([res3.results[c]["x1_out"] for c in range(NCORE)])

    nc4 = _prog("ffn0", build_ffn0)
    maps4 = stage_ffn0_inputs(d, x1_full)
    res4 = run_bass_kernel_spmd(nc4, maps4, core_ids=list(range(NCORE)))
    x2_full = gather_tokens([res4.results[c]["x2_out"] for c in range(NCORE)])

    nc5 = _prog("qkv", build_qkv)
    maps5 = []
    for c in range(NCORE):
        cols = [x2_full[:, G * k:G * (k + 1)] for k in groups_of(c)]
        maps5.append(dict(x2T=np.ascontiguousarray(np.concatenate(cols, axis=1)),
                          w_qkv=np.ascontiguousarray(d["L1_attn_w_qkv"])))
    res5 = run_bass_kernel_spmd(nc5, maps5, core_ids=list(range(NCORE)))
    qkv_cores = [res5.results[c]["qkv_out"].view(bf16) for c in range(NCORE)]

    nc6 = _prog("attn", build_attn)
    maps6 = stage_attn_chain(d, x2_full, qkv_cores)
    res6 = run_bass_kernel_spmd(nc6, maps6, core_ids=list(range(NCORE)))
    attn_full = np.empty((H, T), np.float32)
    for c in range(NCORE):
        a = res6.results[c]["attnT_out"].view(bf16).astype(np.float32)
        for s, h in enumerate([c, 8 + c, 16 + c, 24 + c]):
            attn_full[128 * h:128 * (h + 1)] = a[128 * s:128 * (s + 1)]

    nc7 = _prog("wc", build_wc)
    maps7 = []
    for c in range(NCORE):
        cols = np.concatenate([np.arange(G * k, G * (k + 1)) for k in groups_of(c)])
        maps7.append(dict(
            attnT=np.ascontiguousarray(attn_full[:, cols]).astype(bf16),
            w_c=np.ascontiguousarray(d["L1_attn_w_c"]),
            x2T=np.ascontiguousarray(x2_full[:, cols]),
            ones_col=np.ones((128, 1), np.float32),
            ones_row=np.ones((1, 128), np.float32),
        ))
    res7 = run_bass_kernel_spmd(nc7, maps7, core_ids=list(range(NCORE)))
    x3_full = gather_tokens([res7.results[c]["x3_out"] for c in range(NCORE)])

    nc8 = _prog("ffn1", build_ffn1)
    common = dict(
        rwT1=np.ascontiguousarray(d["L1_ffn_router_weight"].T),
        rb1=np.ascontiguousarray(d["L1_ffn_router_bias"].reshape(64, 1)),
        ebias=np.ascontiguousarray(d["L1_ffn_expert_bias"]),
        identF=np.eye(128, dtype=np.float32),
        ones_col=np.ones((128, 1), np.float32),
        ones_row=np.ones((1, 128), np.float32),
        onesrb=np.ones((1, 128), np.float32).astype(bf16),
    )
    maps8 = []
    for c in range(NCORE):
        cols = np.concatenate([np.arange(G * k, G * (k + 1)) for k in groups_of(c)])
        maps8.append(dict(common, x3T=np.ascontiguousarray(x3_full[:, cols])))
    res8 = run_bass_kernel_spmd(nc8, maps8, core_ids=list(range(NCORE)))
    x4_full = gather_tokens([res8.results[c]["x4_out"] for c in range(NCORE)])

    return np.ascontiguousarray(x4_full.T).reshape(1, T, H).astype(np.float32)
